# version 17
# baseline (speedup 1.0000x reference)
import contextlib
import math
import numpy as np
import ml_dtypes
import concourse.bass as bass
import concourse.mybir as mybir
from concourse.bass_utils import run_bass_kernel_spmd

F32 = mybir.dt.float32
BF16 = mybir.dt.bfloat16
F16 = mybir.dt.float16
I32 = mybir.dt.int32
U8 = mybir.dt.uint8
AF = mybir.ActivationFunctionType
ALU = mybir.AluOpType
AX = mybir.AxisListType
NPBF = ml_dtypes.bfloat16

P = 128
T = 4096
D = 2048
NT = T // P
ND = D // P
RSQ_HD = 1.0 / math.sqrt(128.0)
NEGBIG = -1e30
EPS_LN = 1e-5

CFG = {
    0: dict(OQ=2048, OK=2048, OV=2048, NH=16, KVH=16, rope=False, window=None,
            I=5632,
            wq="L0_attn_Wq", wk="L0_attn_Wk", wv="L0_attn_Wv", wo="L0_attn_Wo"),
    1: dict(OQ=2048, OK=512, OV=512, NH=16, KVH=4, rope=True, window=512,
            I=8192,
            wq="L1_attn_wq", wk="L1_attn_wk", wv="L1_attn_wv", wo="L1_attn_wo"),
}

DRAM_SPEC = {
    "r0": ([T, D], F32, "input"),
    "r1": ([T, D], F32, "inter"),
    "r2": ([T, D], F32, "inter"),
    "r3": ([T, D], F32, "inter"),
    "out": ([T, D], F32, "output"),
    "ident": ([P, P], BF16, "input"),
    "identf": ([P, P], F32, "input"),
    "maskC": ([P, P], F32, "input"),
    "maskA": ([P, P], F32, "input"),
    "ropeC": ([P, NT * 64], F32, "input"),
    "ropeS": ([P, NT * 64], F32, "input"),
    "L0_attn_Wq": ([2048, 2048], F32, "input"),
    "L0_attn_Wk": ([2048, 2048], F32, "input"),
    "L0_attn_Wv": ([2048, 2048], F32, "input"),
    "L0_attn_Wo": ([2048, 2048], F32, "input"),
    "L0_ffn_gate_up_packed": ([11264, 256], I32, "input"),
    "L0_ffn_gate_up_scales": ([16, 11264], F16, "input"),
    "L0_ffn_down_packed": ([2048, 704], I32, "input"),
    "L0_ffn_down_scales": ([44, 2048], F16, "input"),
    "L1_attn_wq": ([2048, 2048], F32, "input"),
    "L1_attn_wk": ([512, 2048], F32, "input"),
    "L1_attn_wv": ([512, 2048], F32, "input"),
    "L1_attn_wo": ([2048, 2048], F32, "input"),
    "L1_ffn_w_gate": ([8192, 2048], F32, "input"),
    "L1_ffn_w_up": ([8192, 2048], F32, "input"),
    "L1_ffn_w_down": ([2048, 8192], F32, "input"),
    "qf0": ([2048, T], BF16, "inter"),
    "kf0": ([2048, T], BF16, "inter"),
    "vt0": ([T, 2048], BF16, "inter"),
    "zf0": ([2048, T], BF16, "inter"),
    "hm0": ([5632, T], BF16, "inter"),
    "wdT0": ([5632, 2048], BF16, "inter"),
    "qf1": ([2048, T], BF16, "inter"),
    "kf1": ([512, T], BF16, "inter"),
    "vt1": ([T, 512], BF16, "inter"),
    "zf1": ([2048, T], BF16, "inter"),
    "hm1": ([8192, T], BF16, "inter"),
    "wdT1": ([8192, 2048], BF16, "inter"),
}

ENGINES = ("sync", "scalar", "vector", "tensor", "gpsimd")


class Sem:
    __slots__ = ("h", "n", "name")

    def __init__(self, handle, name):
        self.h = handle
        self.n = 0
        self.name = name

    def inc(self, k=1):
        self.n += k
        return self.n


class B:

    def __init__(self, mode, reads=(), writes=()):
        self.nc = bass.Bass("TRN2", target_bir_lowering=False)
        self.mode = mode
        self.reads = set(reads)
        self.writes = set(writes)
        self.ops = {e: [] for e in ENGINES}
        self.sems = {}
        self.dram_handles = {}
        self.stack = contextlib.ExitStack()
        self.ext_in = []
        self.ext_out = []

    _UIDS = ("qk0", "qk1", "at0", "at1", "op0", "op1", "gu0", "gu1",
             "dn0", "dn1")

    def sem(self, name):
        if self.mode == "full":
            for u in self._UIDS:
                if name.endswith(u):
                    name = name[:-len(u)] + "X"
                    break
        if name not in self.sems:
            h = self.stack.enter_context(self.nc.semaphore(name))
            self.sems[name] = Sem(h, name)
        return self.sems[name]

    def push_frame(self):
        self._frame = contextlib.ExitStack()

    def pop_frame(self):
        self._frame.close()
        self._frame = None

    def sbuf(self, name, shape, dtype):
        stk = getattr(self, "_frame", None) or self.stack
        return stk.enter_context(self.nc.sbuf_tensor(name, shape, dtype))

    def psum(self, name, shape, dtype):
        stk = getattr(self, "_frame", None) or self.stack
        return stk.enter_context(self.nc.psum_tensor(name, shape, dtype))

    def dram(self, name):
        if name in self.dram_handles:
            return self.dram_handles[name]
        shape, dtype, kind = DRAM_SPEC[name]
        if kind == "input":
            k = "ExternalInput"
            self.ext_in.append(name)
        elif kind == "output":
            k = "ExternalOutput"
            self.ext_out.append(name)
        else:
            if self.mode == "dev":
                if name in self.writes:
                    k = "ExternalOutput"
                    self.ext_out.append(name)
                else:
                    k = "ExternalInput"
                    self.ext_in.append(name)
            else:
                k = "Internal"
        h = self.nc.dram_tensor(name, shape, dtype, kind=k)
        self.dram_handles[name] = h
        return h

    def emit(self, eng, fn):
        self.ops[eng].append(fn)

    def dma(self, eng, out, in_, sem, **kw):
        tgt = sem.inc(16)
        self.emit(eng, lambda e, o=out, i=in_, s=sem.h, kw=kw:
                  e.dma_start(out=o, in_=i, **kw).then_inc(s, 16))
        return tgt

    def wait(self, eng, sem, val):
        self.emit(eng, lambda e, s=sem.h, v=val: e.wait_ge(s, v))

    def waitn(self, eng, sem, val, cyc=384):
        self.emit(eng, lambda e, s=sem.h, v=val: e.wait_ge(s, v))
        self.emit(eng, lambda e, c=cyc: e.nop(cycle_cnt=c))

    def inc_on(self, eng, fn, sem, k=1):
        tgt = sem.inc(k)
        self.emit(eng, lambda e, f=fn, s=sem.h, kk=k: f(e).then_inc(s, kk))
        return tgt

    def op(self, eng, fn):
        self.emit(eng, fn)

    def drain_inc(self, eng, sem, k=1):
        tgt = sem.inc(k)
        self.emit(eng, lambda e, s=sem.h, kk=k: e.drain().then_inc(s, kk))
        return tgt

    def finalize(self):
        with self.nc.Block() as blk:
            for eng in ENGINES:
                def mk(engname):
                    def body(e):
                        for fn in self.ops[engname]:
                            fn(e)
                    return body
                getattr(blk, eng)(mk(eng))
        return self.nc



def emit_consts(b):
    b.identsb = b.sbuf("identsb", [P, P], BF16)
    b.identf = b.sbuf("identfsb", [P, P], F32)
    b.ones1 = b.sbuf("ones1", [1, P], F32)
    b.epsv = b.sbuf("epsv", [P, 1], F32)
    b.m16 = b.sbuf("m16c", [P, 1], F32)
    b.onescol = b.sbuf("onescol", [P, 1], BF16)
    s = b.sem("ldc")
    b.dma("sync", b.identsb[:], b.dram("ident")[:], s)
    b.dma("sync", b.identf[:], b.dram("identf")[:], s)
    cst = b.sem("cst")
    b.wait("vector", s, s.n)
    b.op("vector", lambda e: e.memset(b.ones1[:], 1.0))
    b.op("vector", lambda e: e.memset(b.epsv[:], EPS_LN))
    b.op("vector", lambda e: e.memset(b.m16[:], -16.0))
    b.op("vector", lambda e: e.memset(b.onescol[:], 1.0))
    done = b.drain_inc("vector", cst, 1)
    for eng in ("scalar", "tensor", "gpsimd", "sync"):
        b.wait(eng, cst, done)


def ln_epilogue_stats(b, uid, src_f32, stcol, s_in_ready, in_tgt, s_done):
    junk = b._lnjunk
    b.wait("scalar", s_in_ready, in_tgt)
    b.op("scalar", lambda e: e.activation(junk[:], src_f32, AF.Square,
                                          bias=0.0, scale=1.0,
                                          accum_out=stcol[:, 1:2]))
    ssq_t = b.drain_inc("scalar", b.sem(f"lnsq{uid}"), 1)
    b.wait("vector", s_in_ready, in_tgt)
    b.op("vector", lambda e: e.reduce_sum(stcol[:, 0:1], src_f32, axis=AX.X))
    vaux = b.sem(f"lnvaux{uid}")
    va = b.drain_inc("vector", vaux, 1)
    b.wait("vector", vaux, va)
    b.wait("vector", b.sem(f"lnsq{uid}"), ssq_t)

    def stats1(e):
        e.tensor_scalar(stcol[:, 2:3], stcol[:, 0:1], 1.0 / D, None,
                        op0=ALU.mult)
        e.tensor_scalar(stcol[:, 3:4], stcol[:, 0:1], -1.0 / D, None,
                        op0=ALU.mult)
        e.tensor_scalar(stcol[:, 4:5], stcol[:, 1:2], 1.0 / D, None,
                        op0=ALU.mult)
    b.op("vector", stats1)
    va = b.drain_inc("vector", vaux, 1)
    b.wait("vector", vaux, va)
    b.op("vector", lambda e: e.tensor_tensor(
        stcol[:, 5:6], stcol[:, 2:3], stcol[:, 2:3], op=ALU.mult))
    va = b.drain_inc("vector", vaux, 1)
    b.wait("vector", vaux, va)
    b.op("vector", lambda e: e.scalar_tensor_tensor(
        stcol[:, 4:5], stcol[:, 5:6], -1.0, stcol[:, 4:5],
        op0=ALU.mult, op1=ALU.add))
    vst_t = b.drain_inc("vector", b.sem(f"lnvst{uid}"), 1)
    b.wait("scalar", b.sem(f"lnvst{uid}"), vst_t)

    def rstd_chain(e):
        var = stcol[:, 4:5]
        rstd = stcol[:, 5:6]
        e.activation(rstd, var, AF.Ln, bias=b.epsv[:], scale=1.0)
        e.activation(rstd, rstd, AF.Exp, bias=0.0, scale=-0.5)
    b.op("scalar", rstd_chain)
    aux = b.sem(f"lnaux{uid}")
    aux_t = b.drain_inc("scalar", aux, 1)
    b.wait("scalar", aux, aux_t)
    b.op("scalar", lambda e: e.activation(
        stcol[:, 3:4], stcol[:, 3:4], AF.Identity, bias=0.0,
        scale=stcol[:, 5:6]))
    return b.drain_inc("scalar", s_done, 1)


def emit_ln_to_hfm(b, L, xname, hfm, uid, tpb=None):
    x = b.dram(xname)
    xa = [b.sbuf(f"lnx{uid}_0", [P, D], F32)] * 2
    xnb = [b.sbuf(f"lnxn{uid}_{i}", [P, D], BF16) for i in range(2)]
    if getattr(b, "_lnjunk", None) is None:
        b._lnjunk = b.sbuf(f"lnj{uid}", [P, D], F32)
    st = b.sbuf(f"lnst{uid}", [P, 12], F32)
    if tpb is None:
        tpb = TPB(b, uid)
    s_ldx = b.sem(f"ldx{uid}")
    s_xn = b.sem(f"xn{uid}")
    s_tr = tpb.s_tr
    s_evh = tpb.s_ev
    s_std = b.sem(f"std{uid}")
    ldx0, xn0, tr0, evh0, std0 = s_ldx.n, s_xn.n, s_tr.n, s_evh.n, s_std.n
    hview = hfm[:].rearrange("p (c t) -> p c t", t=T)

    for tt in range(NT):
        bi = tt % 2
        stcol = st[:, bi * 6:bi * 6 + 6]
        if tt >= 1:
            b.wait("sync", s_xn, xn0 + tt)
        ldt = b.dma("sync", xa[bi][:], x[tt * P:(tt + 1) * P, :], s_ldx)
        if tt >= 2:
            b.wait("scalar", s_xn, xn0 + tt - 1)
            b.wait("vector", s_xn, xn0 + tt - 1)
        std_t = ln_epilogue_stats(b, uid, xa[bi][:], stcol, s_ldx, ldt, s_std)
        b.wait("scalar", s_std, std_t)
        if tt >= 2:
            b.wait("scalar", s_tr, tr0 + (tt - 1) * 2)
        b.op("scalar", lambda e, xb=xa[bi], ob=xnb[bi], sc=stcol:
             e.activation(ob[:], xb[:], AF.Identity, bias=sc[:, 3:4],
                          scale=sc[:, 5:6]))
        b.drain_inc("scalar", s_xn, 1)
        b.wait("tensor", s_xn, xn0 + tt + 1)
        for g0 in (0, 8):
            srcs = [xnb[bi][:, (g0 + i) * P:(g0 + i + 1) * P]
                    for i in range(8)]
            dstb = hview[:, g0:g0 + 8, tt * P:(tt + 1) * P]
            tpb.group(srcs, dstb)
    return s_evh, evh0 + NT * 2


class TPB:

    def __init__(self, b, uid):
        self.b = b
        self.banks = [b.psum(f"tpb{uid}_{i}", [P, 8 * P], BF16)
                      for i in range(2)]
        self.s_tr = b.sem(f"tpbtr{uid}")
        self.s_ev = b.sem(f"tpbev{uid}")
        self.g = 0

    def group(self, srcs, dst_batched, gate=None, ev_eng="vector",
              ev_scale=None):
        b = self.b
        n = len(srcs)
        bank = self.banks[self.g % 2]
        if self.g >= 2:
            b.wait("tensor", self.s_ev, self.s_ev.n - 1)
        for i, s in enumerate(srcs):
            fn = (lambda e, o=bank[:, i * P:(i + 1) * P], i_=s:
                  e.transpose(o, i_, b.identsb[:]))
            if i == n - 1:
                b.inc_on("tensor", fn, self.s_tr, 1)
            else:
                b.op("tensor", fn)
        b.waitn(ev_eng, self.s_tr, self.s_tr.n)
        if gate is not None:
            b.wait(ev_eng, gate[0], gate[1])
        if ev_eng == "vector":
            b.op("vector", lambda e, o=dst_batched, i=bank[:, 0:n * P]:
                 e.tensor_copy(o, i))
        else:
            sc = 1.0 if ev_scale is None else ev_scale
            b.op("scalar", lambda e, o=dst_batched, i=bank[:, 0:n * P],
                 s=sc: e.activation(o, i, AF.Copy, bias=0.0, scale=s))
        tgt = b.drain_inc(ev_eng, self.s_ev, 1)
        self.g += 1
        return self.s_ev, tgt


class WT:

    def __init__(self, b, uid, tpb, ncols=D):
        self.b = b
        self.uid = uid
        self.tpb = tpb
        self.ncols = ncols
        self.ndc = ncols // P
        self.wsb = [b.sbuf(f"wt{uid}_{i}", [P, ncols], BF16) for i in range(2)]
        self.s_ldw = b.sem(f"wtld{uid}")
        self.s_evw = tpb.s_ev
        self.ctr = 0

    def prep(self, wname, row0, nrows, dst, dst_off, ostride,
             gate=None):
        b = self.b
        W = b.dram(wname)
        NDC = self.ndc
        nj = nrows // P
        ngroups_per_j = (NDC + 7) // 8
        for j in range(nj):
            bi = self.ctr % 2
            if self.ctr >= 2:
                b.wait("gpsimd", self.tpb.s_tr,
                       self.tpb.s_tr.n - ngroups_per_j)
            ldt = b.dma("gpsimd", self.wsb[bi][:],
                        W[row0 + j * P: row0 + (j + 1) * P, :], self.s_ldw)
            b.wait("tensor", self.s_ldw, ldt)
            for g0 in range(0, NDC, 8):
                n = min(8, NDC - g0)
                srcs = [self.wsb[bi][:, (g0 + i) * P:(g0 + i + 1) * P]
                        for i in range(n)]
                r0 = dst_off % ostride + j * P
                if ostride == P:
                    off = dst_off + g0 * ostride + j * P
                    dstb = dst[:, off:off + n * P]
                else:
                    c0 = g0 + dst_off // ostride
                    dstb = dst[:].rearrange("p (c q) -> p c q",
                                            q=ostride)[:, c0:c0 + n,
                                                       r0:r0 + P]
                self.tpb.group(srcs, dstb, gate=gate)
                gate = None
            self.ctr += 1
        return self.s_evw, self.s_evw.n



def ph_lnqkv(b, L):
    cfg = CFG[L]
    uid = f"qk{L}"
    hfm = b.sbuf(f"hfm{uid}", [P, ND * T], BF16)
    tpb = TPB(b, uid)
    s_evh, evh_tgt = emit_ln_to_hfm(b, L, "r0" if L == 0 else "r2", hfm, uid,
                                    tpb=tpb)
    b.wait("tensor", s_evh, evh_tgt)

    wt = WT(b, uid, tpb)
    s_mm = b.sem(f"mm{uid}")
    s_ev = b.sem(f"ev{uid}")
    s_st = b.sem(f"st{uid}")
    b.store_sems.add(s_st)

    if L == 0:
        wTb = [b.sbuf(f"wT{uid}_{i}", [P, ND * P], BF16) for i in range(2)]
        psq = [b.psum(f"psq{uid}_{i}", [P, 512], F32) for i in range(4)]
        qst = [b.sbuf(f"qst{uid}_{i}", [P, 512], BF16) for i in range(2)]
        ev_ctr = 0
        for (wname, outname, scale) in ((cfg["wq"], "qf0", RSQ_HD),
                                        (cfg["wk"], "kf0", 1.0)):
            outd = b.dram(outname)
            for oc in range(16):
                pb = oc % 2
                gate = (s_mm, s_mm.n) if wt.ctr >= 2 else None
                _, evw_tgt = wt.prep(wname, oc * P, P, wTb[pb], 0, P, gate=gate)
                b.wait("tensor", wt.s_evw, evw_tgt)
                for tg in range(8):
                    bank = psq[ev_ctr % 4]
                    if ev_ctr >= 4:
                        b.wait("tensor", s_ev, s_ev.n - 3)
                    for dc in range(ND):
                        rhs = hfm[:, dc * T + tg * 512:
                                  dc * T + (tg + 1) * 512]
                        lhsT = wTb[pb][:, dc * P:(dc + 1) * P]
                        fn = (lambda e, o=bank[:], l=lhsT, r=rhs,
                              st=(dc == 0), sp=(dc == ND - 1):
                              e.matmul(o, l, r, start=st, stop=sp))
                        if dc == ND - 1:
                            b.inc_on("tensor", fn, s_mm, 1)
                        else:
                            b.op("tensor", fn)
                    b.waitn("scalar", s_mm, s_mm.n)
                    sb = ev_ctr % 2
                    if ev_ctr >= 2:
                        b.wait("scalar", s_st, s_st.n - 16)
                    b.op("scalar", lambda e, o=qst[sb][:], i=bank[:],
                         sc=scale: e.activation(o, i, AF.Copy, bias=0.0,
                                                scale=sc))
                    b.drain_inc("scalar", s_ev, 1)
                    b.wait("gpsimd", s_ev, s_ev.n)
                    b.dma("gpsimd", outd[oc * P:(oc + 1) * P,
                                         tg * 512:(tg + 1) * 512],
                          qst[sb][:], s_st)
                    ev_ctr += 1
        wvTs = b.sbuf(f"wvT{uid}", [P, ND * 512], BF16)
        psv = [b.psum(f"psv{uid}_{i}", [P, 512], F32) for i in range(2)]
        vst = [b.sbuf(f"vst{uid}_{i}", [P, 512], BF16) for i in range(2)]
        outd = b.dram("vt0")
        s_mmv = b.sem(f"mmv{uid}")
        s_evv = b.sem(f"evv{uid}")
        s_stv = b.sem(f"stv{uid}")
        b.store_sems.add(s_stv)
        for nc4 in range(4):
            gate = (s_mmv, s_mmv.n) if nc4 > 0 else None
            _, evw_tgt = wt.prep(cfg["wv"], nc4 * 512, 512, wvTs, 0, 512,
                                 gate=gate)
            b.wait("tensor", wt.s_evw, evw_tgt)
            for tt in range(NT):
                pb = tt % 2
                if tt >= 2 or nc4 > 0:
                    b.wait("tensor", s_evv, max(0, s_evv.n - 1))
                for dc in range(ND):
                    lhsT = hfm[:, dc * T + tt * P: dc * T + (tt + 1) * P]
                    rhs = wvTs[:, dc * 512:(dc + 1) * 512]
                    fn = (lambda e, o=psv[pb][:], l=lhsT, r=rhs,
                          st=(dc == 0), sp=(dc == ND - 1):
                          e.matmul(o, l, r, start=st, stop=sp))
                    if dc == ND - 1:
                        b.inc_on("tensor", fn, s_mmv, 1)
                    else:
                        b.op("tensor", fn)
                b.waitn("scalar", s_mmv, s_mmv.n)
                if tt >= 2 or nc4 > 0:
                    b.wait("scalar", s_stv, max(0, s_stv.n - 16))
                b.op("scalar", lambda e, o=vst[pb][:], i=psv[pb][:]:
                     e.activation(o, i, AF.Copy, bias=0.0, scale=1.0))
                b.drain_inc("scalar", s_evv, 1)
                b.wait("gpsimd", s_evv, s_evv.n)
                b.dma("gpsimd", outd[tt * P:(tt + 1) * P,
                                     nc4 * 512:(nc4 + 1) * 512],
                      vst[pb][:], s_stv)
    else:
        ropeC = b.sbuf(f"ropeC{uid}", [P, NT * 64], F32)
        ropeS = b.sbuf(f"ropeS{uid}", [P, NT * 64], F32)
        s_ldc2 = b.sem(f"ldc2{uid}")
        b.dma("sync", ropeC[:], b.dram("ropeC")[:], s_ldc2)
        b.dma("sync", ropeS[:], b.dram("ropeS")[:], s_ldc2)
        b.wait("vector", s_ldc2, s_ldc2.n)

        wTs = b.sbuf(f"wTs{uid}", [P, ND * 512], BF16)
        psp = [b.psum(f"psp{uid}_{i}", [P, 512], F32) for i in range(2)]
        rtmp = b._lnjunk
        ropebf = [b.sbuf(f"rbf{uid}_{i}", [P, 512], BF16) for i in range(2)]
        hstage = [b.sbuf(f"hst{uid}_{i}", [P, 4 * P], BF16) for i in range(2)]
        vstage = ropebf
        s_mmp = b.sem(f"mmp{uid}")
        s_rp = b.sem(f"rp{uid}")
        s_trh = b.sem(f"trh{uid}")
        s_evh2 = b.sem(f"evh2{uid}")
        s_sth = b.sem(f"sth{uid}")
        b.store_sems.add(s_sth)
        tr_ctr = 0
        bank_gate = [None, None]
        for (wname, O, outname, isq) in ((cfg["wq"], 2048, "qf1", True),
                                         (cfg["wk"], 512, "kf1", True),
                                         (cfg["wv"], 512, "vt1", False)):
            outd = b.dram(outname)
            for nc4 in range(O // 512):
                gate = ((s_mmp, s_mmp.n) if s_mmp.n > 0 else None)
                _, evw_tgt = wt.prep(wname, nc4 * 512, 512, wTs, 0, 512,
                                     gate=gate)
                b.wait("tensor", wt.s_evw, evw_tgt)
                for tt in range(NT):
                    pb = tt % 2
                    if bank_gate[pb] is not None:
                        b.wait("tensor", bank_gate[pb][0], bank_gate[pb][1])
                    for dc in range(ND):
                        lhsT = hfm[:, dc * T + tt * P: dc * T + (tt + 1) * P]
                        rhs = wTs[:, dc * 512:(dc + 1) * 512]
                        fn = (lambda e, o=psp[pb][:], l=lhsT, r=rhs,
                              st=(dc == 0), sp=(dc == ND - 1):
                              e.matmul(o, l, r, start=st, stop=sp))
                        if dc == ND - 1:
                            b.inc_on("tensor", fn, s_mmp, 1)
                        else:
                            b.op("tensor", fn)
                    if isq:
                        b.waitn("vector", s_mmp, s_mmp.n)
                        def rope1(e, ps=psp[pb], tt=tt):
                            cs = ropeC[:, tt * 64:(tt + 1) * 64]
                            sn = ropeS[:, tt * 64:(tt + 1) * 64]
                            for j in range(4):
                                q1 = ps[:, j * P:j * P + 64]
                                q2 = ps[:, j * P + 64:(j + 1) * P]
                                o = rtmp[:, j * 256:(j + 1) * 256]
                                e.tensor_tensor(o[:, 0:64], q1, cs, op=ALU.mult)
                                e.tensor_tensor(o[:, 64:128], q2, sn, op=ALU.mult)
                                e.tensor_tensor(o[:, 128:192], q1, sn, op=ALU.mult)
                                e.tensor_tensor(o[:, 192:256], q2, cs, op=ALU.mult)
                        b.op("vector", rope1)
                        rt = b.drain_inc("vector", s_rp, 1)
                        b.wait("vector", s_rp, rt)
                        def rope2(e, ob=ropebf[pb]):
                            for j in range(4):
                                o = rtmp[:, j * 256:(j + 1) * 256]
                                e.tensor_tensor(ob[:, j * P:j * P + 64],
                                                o[:, 0:64], o[:, 64:128],
                                                op=ALU.subtract)
                                e.tensor_tensor(ob[:, j * P + 64:(j + 1) * P],
                                                o[:, 128:192], o[:, 192:256],
                                                op=ALU.add)
                        b.op("vector", rope2)
                        b.drain_inc("vector", s_rp, 1)
                        bank_gate[pb] = (s_rp, s_rp.n)
                        b.wait("tensor", s_rp, s_rp.n)
                        sb = tr_ctr % 2
                        gate_h = ((s_sth, s_sth.n - 16)
                                  if tr_ctr >= 2 else None)
                        srcs = [ropebf[pb][:, j * P:(j + 1) * P]
                                for j in range(4)]
                        _, ev2 = tpb.group(
                            srcs, hstage[sb][:], gate=gate_h,
                            ev_eng="scalar",
                            ev_scale=(RSQ_HD if wname == cfg["wq"] else 1.0))
                        b.wait("gpsimd", tpb.s_ev, ev2)
                        b.dma("gpsimd",
                              outd[nc4 * 4 * P:(nc4 * 4 + 4) * P,
                                   tt * P:(tt + 1) * P].rearrange(
                                       "(j p) t -> p j t", p=P),
                              hstage[sb][:].rearrange("p (j t) -> p j t", t=P),
                              s_sth)
                        tr_ctr += 1
                    else:
                        b.waitn("scalar", s_mmp, s_mmp.n)
                        sb = tt % 2
                        if tt >= 2:
                            b.wait("scalar", s_sth, s_sth.n - 16)
                        b.op("scalar", lambda e, o=vstage[sb][:], i=psp[pb][:]:
                             e.activation(o, i, AF.Copy, bias=0.0, scale=1.0))
                        b.drain_inc("scalar", s_evh2, 1)
                        bank_gate[pb] = (s_evh2, s_evh2.n)
                        b.wait("gpsimd", s_evh2, s_evh2.n)
                        b.dma("gpsimd", outd[tt * P:(tt + 1) * P, 0:512],
                              vstage[sb][:], s_sth)



def wdT_state(b, L, tpb, uid):
    st = {}
    st["s_stw"] = b.sem(f"stw{uid}")
    b.store_sems.add(st["s_stw"])
    if L == 0:
        st["dq"] = DQ(b, uid, "L0_ffn_down_packed", "L0_ffn_down_scales",
                      5632, 44, tpb=tpb)
        st["scTd"] = b.sbuf(f"scTd{uid}", [P, 16 * 44], F32)
        st["dq"].prep_scales(st["scTd"], 16)
        st["icn"] = 44
    else:
        st["wt"] = WT(b, uid, tpb, ncols=8192)
        st["icn"] = 64
    st["slab"] = b.sbuf(f"slab{uid}", [P, st["icn"] * P], BF16)
    st["n"] = 0
    return st


def wdT_tile(b, L, st, wdTd):
    i = st["n"]
    st["n"] += 1
    s_stw = st["s_stw"]
    gate = (s_stw, s_stw.n) if i >= 1 else None
    if L == 0:
        _, ev = st["dq"].prep(i * P, st["scTd"], i * 44, st["slab"], 0, P,
                              gate=gate)
        evsem = st["dq"].tpb.s_ev
    else:
        _, ev = st["wt"].prep("L1_ffn_w_down", i * P, P, st["slab"], 0, P,
                              gate=gate)
        evsem = st["wt"].s_evw
    b.wait("gpsimd", evsem, ev)
    for ic in range(st["icn"]):
        b.dma("gpsimd", wdTd[ic * P:(ic + 1) * P, i * P:(i + 1) * P],
              st["slab"][:, ic * P:(ic + 1) * P], s_stw)
    return (s_stw, s_stw.n)


def ph_attn(b, L):
    cfg = CFG[L]
    uid = f"at{L}"
    NH = cfg["NH"]
    RPG = NH // cfg["KVH"]
    window = cfg["window"]
    qf = b.dram(f"qf{L}")
    kf = b.dram(f"kf{L}")
    vt = b.dram(f"vt{L}")
    zf = b.dram(f"zf{L}")

    maskC = b.sbuf(f"mc{uid}", [P, P], F32)
    maskA = b.sbuf(f"ma{uid}", [P, P], F32)
    s_ldm = b.sem(f"ldm{uid}")
    b.dma("sync", maskC[:], b.dram("maskC")[:], s_ldm)
    b.dma("sync", maskA[:], b.dram("maskA")[:], s_ldm)
    mask_tgt = s_ldm.n
    b.wait("vector", s_ldm, mask_tgt)

    Qs = [b.sbuf(f"Qs{uid}_{i}", [P, T], BF16) for i in range(2)]
    Ks = [b.sbuf(f"Ks{uid}_{i}", [P, T], BF16) for i in range(2)]
    Vs = [b.sbuf(f"Vs{uid}_{i}", [P, T], BF16) for i in range(2)]
    PTs = [b.sbuf(f"PT{uid}_{i}", [P, 512], BF16) for i in range(4)]
    denrowb = [b.sbuf(f"dr{uid}_{i}", [1, 512], F32) for i in range(2)]
    dn4b = [b.sbuf(f"dn4{uid}_{i}", [P, 4], F32) for i in range(2)]
    rec4b = [b.sbuf(f"rc4{uid}_{i}", [P, 4], F32) for i in range(2)]
    rrowb = [b.sbuf(f"rr{uid}_{i}", [1, 512], F32) for i in range(2)]
    rbstb = [b.sbuf(f"rb{uid}_{i}", [P, 512], F32) for i in range(2)]
    zst = [b.sbuf(f"zs{uid}_{i}", [P, 512], BF16) for i in range(2)]

    psS = [b.psum(f"psS{uid}_{i}", [P, 512], F32) for i in range(2)]
    psZ = [b.psum(f"psZ{uid}_{i}", [P, 512], F32) for i in range(2)]
    psRb = [b.psum(f"psR{uid}_{i}", [P, 512], F32) for i in range(2)]
    if not hasattr(b, "dbg"):
        b.dbg = {}
    b.dbg.update({f"denrow{uid}": denrowb[0], f"dn4{uid}": dn4b[0],
                  f"rec4{uid}": rec4b[0], f"rrow{uid}": rrowb[0],
                  f"rbst{uid}": rbstb[0]})

    tpbA = TPB(b, uid)
    wdTd = b.dram(f"wdT{L}")
    wst = wdT_state(b, L, tpbA, uid)
    s_ldq = b.sem(f"ldq{uid}")
    s_mmS = b.sem(f"mmS{uid}")
    s_msk = b.sem(f"msk{uid}")
    s_exp = b.sem(f"exp{uid}")
    s_mmZ = b.sem(f"mmZ{uid}")
    s_dev = b.sem(f"dev{uid}")
    s_tr2 = b.sem(f"tr2{uid}")
    s_rec = b.sem(f"rec{uid}")
    s_bc = b.sem(f"bc{uid}")
    s_evrb = b.sem(f"evrb{uid}")
    s_z = b.sem(f"z{uid}")
    s_stz = b.sem(f"stz{uid}")
    b.store_sems.add(s_stz)

    qgate = [None, None]
    kgate = [None, None]
    vgate = [None, None]
    psS_gate = [None, None]
    psZ_gate = [None, None]
    pts_gate = [None, None, None, None]
    psR_gate = [None, None]
    kvloaded = -1
    stctr = 0

    for h in range(NH):
        bq = h % 2
        if qgate[bq] is not None:
            b.wait("sync", qgate[bq][0], qgate[bq][1])
        b.dma("sync", Qs[bq][:], qf[h * P:(h + 1) * P, :], s_ldq)
        kv = h // RPG
        bkv = kv % 2
        if kv != kvloaded:
            if kgate[bkv] is not None:
                b.wait("sync", kgate[bkv][0], kgate[bkv][1])
            b.dma("sync", Ks[bkv][:], kf[kv * P:(kv + 1) * P, :], s_ldq)
            if vgate[bkv] is not None:
                b.wait("sync", vgate[bkv][0], vgate[bkv][1])
            b.dma("sync",
                  Vs[bkv][:].rearrange("p (n o) -> p n o", o=P),
                  vt[:, kv * P:(kv + 1) * P].rearrange("(n p) o -> p n o", p=P),
                  s_ldq)
            kvloaded = kv
        ld_tgt = s_ldq.n
        b.wait("tensor", s_ldq, ld_tgt)

        for g in range(8):
            zb = g % 2
            psR = psRb[zb]
            denrow, dn4, rec4, rrow, rbst = (denrowb[zb], dn4b[zb],
                                             rec4b[zb], rrowb[zb], rbstb[zb])
            if psZ_gate[zb] is not None:
                b.wait("tensor", psZ_gate[zb][0], psZ_gate[zb][1])
            if psR_gate[zb] is not None:
                b.wait("tensor", psR_gate[zb][0], psR_gate[zb][1])
            stlo = 0 if not window else max(0, 4 * g - 4)
            sthi = 4 * g + 4
            first = True
            pend = []
            sts = [st for st in range(stlo, sthi)
                   if max(0, st - 4 * g) <= (
                       3 if not window else min(3, st + 4 - 4 * g))]
            for sti, st in enumerate(sts):
                j0 = max(0, st - 4 * g)
                j1 = 3 if not window else min(3, st + 4 - 4 * g)
                pb = stctr % 2
                sb = stctr % 4
                stctr += 1
                if psS_gate[pb] is not None:
                    b.wait("tensor", psS_gate[pb][0], psS_gate[pb][1])
                b.inc_on("tensor",
                         lambda e, o=psS[pb][:, j0 * P:(j1 + 1) * P],
                         l=Ks[bkv][:, st * P:(st + 1) * P],
                         r=Qs[bq][:, (4 * g + j0) * P:(4 * g + j1 + 1) * P]:
                         e.matmul(o, l, r, start=True, stop=True),
                         s_mmS, 1)
                need_mask = []
                if 4 * g <= st <= 4 * g + 3 and j0 <= st - 4 * g <= j1:
                    need_mask.append(((st - 4 * g) * P, maskA))
                if window and j0 <= st + 4 - 4 * g <= j1 and st + 4 <= 4 * g + 3:
                    need_mask.append(((st + 4 - 4 * g) * P, maskC))
                if need_mask:
                    b.waitn("vector", s_mmS, s_mmS.n)
                    for (mo, mt) in need_mask:
                        b.op("vector", lambda e, o=psS[pb][:, mo:mo + P],
                             m=mt: e.tensor_tensor(o, o, m[:], op=ALU.add))
                    b.drain_inc("vector", s_msk, 1)
                    b.wait("scalar", s_msk, s_msk.n)
                else:
                    b.waitn("scalar", s_mmS, s_mmS.n)
                if pts_gate[sb] is not None:
                    b.wait("scalar", pts_gate[sb][0], pts_gate[sb][1])
                b.op("scalar", lambda e, o=PTs[sb][:, j0 * P:(j1 + 1) * P],
                     i=psS[pb][:, j0 * P:(j1 + 1) * P]:
                     e.activation(o, i, AF.Exp, bias=b.m16[:], scale=1.0))
                pend.append((st, j0, j1, sb))
                psS_gate[pb] = (s_exp, s_exp.n + 1)
                if len(pend) == 2 or sti == len(sts) - 1:
                    b.drain_inc("scalar", s_exp, 1)
                    b.wait("tensor", s_exp, s_exp.n)
                    for (pst, pj0, pj1, psb) in pend:
                        b.op("tensor",
                             lambda e, o=psR[0:1, pj0 * P:(pj1 + 1) * P],
                             l=b.onescol[:],
                             r=PTs[psb][:, pj0 * P:(pj1 + 1) * P],
                             stf=first, sp=(pst == sts[-1]):
                             e.matmul(o, l, r, start=stf, stop=sp))
                        b.inc_on("tensor",
                                 lambda e, o=psZ[zb][:, pj0 * P:(pj1 + 1) * P],
                                 l=Vs[bkv][:, pst * P:(pst + 1) * P],
                                 r=PTs[psb][:, pj0 * P:(pj1 + 1) * P],
                                 stf=first, sp=(pst == sts[-1]):
                                 e.matmul(o, l, r, start=stf, stop=sp),
                                 s_mmZ, 1)
                        pts_gate[psb] = (s_mmZ, s_mmZ.n)
                        first = False
                    pend = []
            b.waitn("vector", s_mmZ, s_mmZ.n)
            b.op("vector", lambda e, o=denrow, i=psR:
                 e.tensor_copy(o[:], i[0:1, 0:512]))
            dev_t = b.drain_inc("vector", s_dev, 1)
            b.wait("tensor", s_dev, dev_t)
            for j in range(4):
                fn = (lambda e, o=psR[:, 508 + j:509 + j],
                      i=denrow[0:1, j * P:(j + 1) * P]:
                      e.transpose(o, i, b.identf[0:1, 0:1]))
                if j == 3:
                    b.inc_on("tensor", fn, s_tr2, 1)
                else:
                    b.op("tensor", fn)
            b.waitn("vector", s_tr2, s_tr2.n)
            b.op("vector", lambda e, o=dn4, i=psR:
                 e.tensor_copy(o[:], i[:, 508:512]))
            dv2 = b.drain_inc("vector", s_dev, 1)
            b.wait("vector", s_dev, dv2)
            b.op("vector", lambda e, o=rec4, i=dn4: e.reciprocal(o[:], i[:]))
            rec_t = b.drain_inc("vector", s_rec, 1)
            b.wait("tensor", s_rec, rec_t)
            for j in range(4):
                fn = (lambda e, o=psR[0:1, j * P:(j + 1) * P],
                      i=rec4[:, j:j + 1]:
                      e.transpose(o, i, b.identf[:]))
                if j == 3:
                    b.inc_on("tensor", fn, s_tr2, 1)
                else:
                    b.op("tensor", fn)
            b.waitn("vector", s_tr2, s_tr2.n)
            b.op("vector", lambda e, o=rrow, i=psR:
                 e.tensor_copy(o[:], i[0:1, 0:512]))
            rrw_t = b.drain_inc("vector", s_dev, 1)
            b.wait("tensor", s_dev, rrw_t)
            b.inc_on("tensor", lambda e, o=psR, r=rrow:
                     e.matmul(o[:], b.ones1[:], r[0:1, :],
                              start=True, stop=True), s_bc, 1)
            b.waitn("scalar", s_bc, s_bc.n)
            b.op("scalar", lambda e, o=rbst, i=psR:
                 e.activation(o[:], i[:], AF.Copy, bias=0.0, scale=1.0))
            b.drain_inc("scalar", s_evrb, 1)
            psR_gate[zb] = (s_evrb, s_evrb.n)
            b.wait("vector", s_evrb, s_evrb.n)
            zsb = g % 2
            if s_stz.n >= 32:
                b.wait("vector", s_stz, s_stz.n - 16)
            b.op("vector", lambda e, o=zst[zsb][:], a=psZ[zb][:], m=rbst[:]:
                 e.tensor_tensor(o, a, m, op=ALU.mult))
            b.drain_inc("vector", s_z, 1)
            psZ_gate[zb] = (s_z, s_z.n)
            b.wait("gpsimd", s_z, s_z.n)
            b.dma("gpsimd", zf[h * P:(h + 1) * P, g * 512:(g + 1) * 512],
                  zst[zsb][:], s_stz)
        qgate[bq] = (s_mmS, s_mmS.n)
        kgate[bkv] = (s_mmS, s_mmS.n)
        vgate[bkv] = (s_mmZ, s_mmZ.n)
        if wst["n"] < 16:
            ready = wdT_tile(b, L, wst, wdTd)
    b._wdT_ready = getattr(b, "_wdT_ready", {})
    b._wdT_ready[L] = ready



def ph_outproj(b, L):
    cfg = CFG[L]
    uid = f"op{L}"
    zf = b.dram(f"zf{L}")
    xin = b.dram("r0" if L == 0 else "r2")
    xout = b.dram("r1" if L == 0 else "r3")

    woT = b.sbuf(f"woT{uid}", [P, 16 * 2048], BF16)
    if not hasattr(b, "dbg"):
        b.dbg = {}
    b.dbg[f"woT{uid}"] = woT
    tpb = TPB(b, uid)
    wt = WT(b, uid, tpb)
    for rr in range(16):
        wt.prep(cfg["wo"], rr * P, P, woT, rr * P, 2048)
    b.wait("tensor", wt.s_evw, wt.s_evw.n)

    zcol = [b.sbuf(f"zc{uid}_{i}", [P, ND * P], BF16) for i in range(2)]
    yrow = [b.sbuf(f"yr{uid}_{i}", [P, D], F32) for i in range(2)]
    xrow = [b.sbuf(f"xr{uid}_{i}", [P, D], F32) for i in range(2)]
    if getattr(b, "_lnjunk", None) is None:
        b._lnjunk = b.sbuf(f"j{uid}", [P, D], F32)
    st = b.sbuf(f"st{uid}", [P, 12], F32)
    psY = [b.psum(f"psY{uid}_{i}", [P, 512], F32) for i in range(4)]

    s_ldz = b.sem(f"ldz{uid}")
    s_ldxr = b.sem(f"ldxr{uid}")
    s_mmY = b.sem(f"mmY{uid}")
    s_evYr = b.sem(f"evYr{uid}")
    s_evYd = b.sem(f"evYd{uid}")
    s_std2 = b.sem(f"std2{uid}")
    s_nrm = b.sem(f"nrm{uid}")
    s_res = b.sem(f"res{uid}")
    s_sto = b.sem(f"sto{uid}")
    b.store_sems.add(s_sto)

    zgate = [None, None]
    ygate = [None, None]
    xgate = [None, None]
    yunit = 0
    for tt in range(NT):
        bi = tt % 2
        stcol = st[:, bi * 6:bi * 6 + 6]
        if zgate[bi] is not None:
            b.wait("sync", zgate[bi][0], zgate[bi][1])
        ldz_t = b.dma("sync",
                      zcol[bi][:].rearrange("p (a t) -> p a t", t=P),
                      zf[:, tt * P:(tt + 1) * P].rearrange(
                          "(a p) t -> p a t", p=P),
                      s_ldz)
        if xgate[bi] is not None:
            b.wait("sync", xgate[bi][0], xgate[bi][1])
        ldx_t = b.dma("sync", xrow[bi][:], xin[tt * P:(tt + 1) * P, :], s_ldxr)
        b.wait("tensor", s_ldz, ldz_t)
        for dcn in range(4):
            bank = psY[yunit % 4]
            if yunit >= 4:
                b.wait("tensor", s_evYr, s_evYr.n - 3)
            for oc in range(16):
                lhsT = zcol[bi][:, oc * P:(oc + 1) * P]
                rhs = woT[:, oc * 2048 + dcn * 512: oc * 2048 + (dcn + 1) * 512]
                fn = (lambda e, o=bank[:], l=lhsT, r=rhs,
                      stf=(oc == 0), sp=(oc == 15):
                      e.matmul(o, l, r, start=stf, stop=sp))
                if oc == 15:
                    b.inc_on("tensor", fn, s_mmY, 1)
                else:
                    b.op("tensor", fn)
            b.waitn("scalar", s_mmY, s_mmY.n)
            if dcn == 0 and ygate[bi] is not None:
                b.wait("scalar", ygate[bi][0], ygate[bi][1])
            b.op("scalar", lambda e,
                 o=yrow[bi][:, dcn * 512:(dcn + 1) * 512],
                 i=bank[:]:
                 e.activation(o, i, AF.Copy, bias=0.0, scale=1.0))
            b.drain_inc("scalar", s_evYr, 1)
            yunit += 1
        zgate[bi] = (s_mmY, s_mmY.n)
        evd_t = b.drain_inc("scalar", s_evYd, 1)
        std_t = ln_epilogue_stats(b, uid, yrow[bi][:], stcol, s_evYd, evd_t,
                                  s_std2)
        b.wait("scalar", s_std2, std_t)
        b.op("scalar", lambda e, y=yrow[bi], sc=stcol:
             e.activation(y[:], y[:], AF.Identity, bias=sc[:, 3:4],
                          scale=sc[:, 5:6]))
        nrm_t = b.drain_inc("scalar", s_nrm, 1)
        b.wait("vector", s_nrm, nrm_t)
        b.wait("vector", s_ldxr, ldx_t)
        b.op("vector", lambda e, x=xrow[bi], y=yrow[bi]:
             e.tensor_tensor(x[:], x[:], y[:], op=ALU.add))
        b.drain_inc("vector", s_res, 1)
        ygate[bi] = (s_res, s_res.n)
        b.wait("gpsimd", s_res, s_res.n)
        b.dma("gpsimd", xout[tt * P:(tt + 1) * P, :], xrow[bi][:], s_sto)
        xgate[bi] = (s_sto, s_sto.n)



class DQ:

    def __init__(self, b, uid, packed_name, scales_name, dper, ngroups,
                 tpb=None):
        self.b = b
        self.uid = uid
        self.tpb = tpb if tpb is not None else TPB(b, uid + "q")
        self.packed = b.dram(packed_name)
        self.scales = b.dram(scales_name)
        self.dper = dper
        self.C = dper // 2
        self.ngr = ngroups
        self.pk = [b.sbuf(f"dqp{uid}_{i}", [P, dper // 8], I32)
                   for i in range(2)]
        self.bfa = b.sbuf(f"dqa{uid}", [P, self.C], BF16)
        self.bfb = b.sbuf(f"dqb{uid}", [P, self.C], BF16)
        self.bfc = b.sbuf(f"dqc{uid}", [P, self.C], BF16)
        self.bfd = b.sbuf(f"dqd{uid}", [P, self.C], BF16)
        self.u8lo = b.sbuf(f"dql8{uid}", [P, self.C], U8)
        self.u8hi = b.sbuf(f"dqh8{uid}", [P, self.C], U8)
        self.wdeq = b.sbuf(f"dqw{uid}", [P, dper], BF16)
        self.b4m = b.sbuf(f"dq4{uid}", [P, 1], F32)
        self.b12m = b.sbuf(f"dq12{uid}", [P, 1], F32)
        self.sctmp16 = b.sbuf(f"dqs16{uid}", [P, P], F16)
        self.sctmp32 = b.sbuf(f"dqs32{uid}", [P, P], F32)
        self.pssc = self.tpb.banks[0][:].bitcast(F32)
        self.s_ldp = b.sem(f"dqld{uid}")
        self.s_v = b.sem(f"dqv{uid}")
        self.s_a = b.sem(f"dqact{uid}")
        self.s_sc = b.sem(f"dqsc{uid}")
        self.s_tr = b.sem(f"dqtr{uid}")
        self.s_ev = b.sem(f"dqev{uid}")
        self.s_trs = b.sem(f"dqtrs{uid}")
        self.s_evs = b.sem(f"dqevs{uid}")
        self.s_pk = b.sem(f"dqpk{uid}")
        self.s_vg = b.sem(f"dqvg{uid}")
        self.s_gq = b.sem(f"dqgq{uid}")
        self.ctr = 0
        self.tctr = 0
        self.pk_gate = [None, None]
        b.op("vector", lambda e: e.memset(self.b4m[:], -4.0))
        b.op("vector", lambda e: e.memset(self.b12m[:], -12.0))
        ini = b.drain_inc("vector", self.s_v, 1)
        b.wait("scalar", self.s_v, ini)

    def prep_scales(self, scT, n_otiles):
        b = self.b
        prev_trt = None
        for ot in range(n_otiles):
            if prev_trt is not None:
                b.wait("sync", self.s_v, prev_trt)
            ld = b.dma("sync", self.sctmp16[0:self.ngr, :],
                       self.scales[:, ot * P:(ot + 1) * P], self.s_ldp)
            b.wait("vector", self.s_ldp, ld)
            b.op("vector", lambda e: e.tensor_copy(
                self.sctmp32[0:self.ngr, :], self.sctmp16[0:self.ngr, :]))
            trt = b.drain_inc("vector", self.s_v, 1)
            prev_trt = trt
            b.wait("tensor", self.s_v, trt)
            b.wait("tensor", self.s_evs, self.s_evs.n)
            b.inc_on("tensor", lambda e:
                     e.transpose(self.pssc[:, 0:self.ngr],
                                 self.sctmp32[0:self.ngr, :],
                                 b.identf[0:self.ngr, 0:self.ngr]),
                     self.s_trs, 1)
            b.waitn("vector", self.s_trs, self.s_trs.n)
            b.op("vector", lambda e,
                 o=scT[:, ot * self.ngr:(ot + 1) * self.ngr]:
                 e.tensor_copy(o, self.pssc[:, 0:self.ngr]))
            b.drain_inc("vector", self.s_evs, 1)
        b.wait("sync", self.s_evs, self.s_evs.n)
        b.wait("tensor", self.s_evs, self.s_evs.n)

    def prep(self, row0, scT, sc_off, dst, dst_off, ostride, gate=None):
        b = self.b
        pb = self.ctr % 2
        if self.pk_gate[pb] is not None:
            b.wait("sync", self.pk_gate[pb][0], self.pk_gate[pb][1])
        ldt = b.dma("sync", self.pk[pb][:],
                    self.packed[row0:row0 + P, :], self.s_ldp)
        if self.ctr >= 1:
            b.wait("vector", self.tpb.s_tr, self.tpb.s_tr.n)
        b.wait("vector", self.s_ldp, ldt)
        u8 = self.pk[pb][:].bitcast(U8)
        CC = self.C

        def split2(e):
            e.tensor_scalar(self.u8lo[:], u8, 15, None, op0=ALU.bitwise_and)
            e.tensor_scalar(self.u8hi[:], u8, 4, None,
                            op0=ALU.logical_shift_right)
        b.op("vector", split2)
        pkt = b.drain_inc("vector", self.s_pk, 1)
        self.pk_gate[pb] = (self.s_pk, pkt)
        wodd = self.wdeq[:].rearrange("p (n two) -> p two n", two=2)
        for par in range(2):
            src8 = (self.u8lo if par == 0 else self.u8hi)[:]
            b.op("vector", lambda e, s=src8: e.tensor_copy(self.bfa[:], s))
            b.op("vector", lambda e: e.tensor_scalar(
                self.bfb[:], self.bfa[:], 8.0, None, op0=ALU.is_ge))
            b.op("vector", lambda e: e.scalar_tensor_tensor(
                self.bfc[:], self.bfb[:], -8.0, self.bfa[:],
                op0=ALU.mult, op1=ALU.add))
            b.op("vector", lambda e: e.tensor_scalar(
                self.bfb[:], self.bfb[:], -1.0, 0.5,
                op0=ALU.mult, op1=ALU.add))
            vt = b.drain_inc("vector", self.s_v, 1)
            b.wait("scalar", self.s_v, vt)
            b.op("scalar", lambda e: e.activation(
                self.bfa[:], self.bfc[:], AF.Relu, bias=self.b4m[:],
                scale=1.0))
            b.op("scalar", lambda e: e.activation(
                self.bfd[:], self.bfc[:], AF.Relu, bias=self.b12m[:],
                scale=2.0))
            at = b.drain_inc("scalar", self.s_a, 1)
            b.wait("vector", self.s_a, at)
            b.op("vector", lambda e: e.tensor_tensor(
                self.bfc[:], self.bfc[:], self.bfa[:], op=ALU.add))
            b.op("vector", lambda e: e.tensor_tensor(
                self.bfc[:], self.bfc[:], self.bfd[:], op=ALU.add))
            b.op("vector", lambda e, p=par: e.tensor_tensor(
                wodd[:, p, :], self.bfc[:], self.bfb[:], op=ALU.mult))
        for gi in range(self.ngr):
            b.op("vector", lambda e, g=gi: e.tensor_scalar(
                self.wdeq[:, g * P:(g + 1) * P],
                self.wdeq[:, g * P:(g + 1) * P],
                scT[:, sc_off + g:sc_off + g + 1], None, op0=ALU.mult))
        dqt = b.drain_inc("vector", self.s_v, 1)
        b.wait("tensor", self.s_v, dqt)
        NDC = self.dper // P
        for g0 in range(0, NDC, 8):
            n = min(8, NDC - g0)
            srcs = [self.wdeq[:, (g0 + i) * P:(g0 + i + 1) * P]
                    for i in range(n)]
            if ostride == P:
                off = dst_off + g0 * P
                dstb = dst[:, off:off + n * P]
            else:
                c0 = g0 + dst_off // ostride
                dstb = dst[:].rearrange("p (c q) -> p c q",
                                        q=ostride)[:, c0:c0 + n, 0:P]
            self.tpb.group(srcs, dstb, gate=gate)
            gate = None
        self.ctr += 1
        return self.tpb.s_ev, self.tpb.s_ev.n



def ph_lngateup(b, L):
    cfg = CFG[L]
    uid = f"gu{L}"
    I = cfg["I"]
    NOC = I // P
    hmd = b.dram(f"hm{L}")
    hfm = b.sbuf(f"hfm{uid}", [P, ND * T], BF16)
    if not hasattr(b, "dbg"):
        b.dbg = {}
    b.dbg[f"hfm{uid}"] = hfm
    tpb = TPB(b, uid)
    s_evh, tgt = emit_ln_to_hfm(b, L, "r1" if L == 0 else "r3", hfm, uid,
                                tpb=tpb)
    b.wait("tensor", s_evh, tgt)

    quant = (L == 0)
    if quant:
        dq = DQ(b, uid, "L0_ffn_gate_up_packed", "L0_ffn_gate_up_scales",
                2048, 16, tpb=tpb)
        scT = b.sbuf(f"scT{uid}", [P, 88 * 16], F32)
        dq.prep_scales(scT, 88)
    else:
        wt = WT(b, uid, tpb)
    wTg = [b.sbuf(f"wTg{uid}_{i}", [P, ND * P], BF16) for i in range(2)]
    wTu = [b.sbuf(f"wTu{uid}_{i}", [P, ND * P], BF16) for i in range(2)]
    gs = b.sbuf(f"gs{uid}", [P, 2048], BF16)
    hmst = [b.sbuf(f"hms{uid}_{i}", [P, 512], BF16) for i in range(2)]
    psG = [b.psum(f"psG{uid}_{i}", [P, 512], F32) for i in range(2)]
    psU = [b.psum(f"psU{uid}_{i}", [P, 512], F32) for i in range(2)]

    s_mmG = b.sem(f"mmG{uid}")
    s_evGr = b.sem(f"evGr{uid}")
    s_mmU = b.sem(f"mmU{uid}")
    s_evU = b.sem(f"evU{uid}")
    s_sthm = b.sem(f"sthm{uid}")
    b.store_sems.add(s_sthm)

    ggate = None
    ugate = None
    ggs = [None, None]
    wggate = [None, None]
    wugate = [None, None]
    hm_ctr = 0

    def prep_oc(oc):
        wb = oc % 2
        if quant:
            dq.prep(oc * P, scT, oc * 16, wTg[wb], 0, P, gate=wggate[wb])
            _, g2 = dq.prep(I + oc * P, scT, (44 + oc) * 16, wTu[wb], 0, P,
                            gate=wugate[wb])
            return (dq.tpb.s_ev, g2)
        wt.prep("L1_ffn_w_gate", oc * P, P, wTg[wb], 0, P, gate=wggate[wb])
        _, g2 = wt.prep("L1_ffn_w_up", oc * P, P, wTu[wb], 0, P,
                        gate=wugate[wb])
        return (wt.s_evw, g2)

    prep_tgt = prep_oc(0)
    for oc in range(NOC):
        wb = oc % 2
        evsem, g2 = prep_tgt
        b.wait("tensor", evsem, g2)
        if oc + 1 < NOC:
            prep_tgt = prep_oc(oc + 1)
        for q4 in range(4):
            qb = q4 % 2
            if ggate is not None:
                b.wait("tensor", ggate[0], ggate[1])
            for dc in range(ND):
                for tg in range(2):
                    col = (q4 * 2 + tg) * 512
                    rhs = hfm[:, dc * T + col: dc * T + col + 512]
                    lhsT = wTg[wb][:, dc * P:(dc + 1) * P]
                    fn = (lambda e, o=psG[tg][:], l=lhsT, r=rhs,
                          stf=(dc == 0), sp=(dc == ND - 1):
                          e.matmul(o, l, r, start=stf, stop=sp))
                    if dc == ND - 1 and tg == 1:
                        b.inc_on("tensor", fn, s_mmG, 1)
                    else:
                        b.op("tensor", fn)
            if ugate is not None:
                b.wait("tensor", ugate[0], ugate[1])
            for dc in range(ND):
                for tg in range(2):
                    col = (q4 * 2 + tg) * 512
                    rhs = hfm[:, dc * T + col: dc * T + col + 512]
                    lhsT = wTu[wb][:, dc * P:(dc + 1) * P]
                    fn = (lambda e, o=psU[tg][:], l=lhsT, r=rhs,
                          stf=(dc == 0), sp=(dc == ND - 1):
                          e.matmul(o, l, r, start=stf, stop=sp))
                    if dc == ND - 1 and tg == 1:
                        b.inc_on("tensor", fn, s_mmU, 1)
                    else:
                        b.op("tensor", fn)
            b.waitn("scalar", s_mmG, s_mmG.n)
            if ggs[qb] is not None:
                b.wait("scalar", ggs[qb][0], ggs[qb][1])
            for tg in range(2):
                b.op("scalar", lambda e,
                     o=gs[:, qb * 1024 + tg * 512: qb * 1024 + (tg + 1) * 512],
                     i=psG[tg][:]:
                     e.activation(o, i, AF.Silu, bias=0.0, scale=1.0))
            gst = b.drain_inc("scalar", s_evGr, 1)
            ggate = (s_evGr, gst)
            b.waitn("vector", s_mmU, s_mmU.n)
            b.wait("vector", s_evGr, gst)
            for tg in range(2):
                sb = hm_ctr % 2
                if hm_ctr >= 2:
                    b.wait("vector", s_sthm, s_sthm.n - 16)
                b.op("vector", lambda e, o=hmst[sb][:], a=psU[tg][:],
                     m=gs[:, qb * 1024 + tg * 512: qb * 1024 + (tg + 1) * 512]:
                     e.tensor_tensor(o, a, m, op=ALU.mult))
                b.drain_inc("vector", s_evU, 1)
                b.wait("gpsimd", s_evU, s_evU.n)
                b.dma("gpsimd", hmd[oc * P:(oc + 1) * P,
                                    (q4 * 2 + tg) * 512:
                                    (q4 * 2 + tg + 1) * 512],
                      hmst[sb][:], s_sthm)
                hm_ctr += 1
            ugate = (s_evU, s_evU.n)
            ggs[qb] = (s_evU, s_evU.n)
        wggate[wb] = (s_mmG, s_mmG.n)
        wugate[wb] = (s_mmU, s_mmU.n)



def ph_down(b, L):
    cfg = CFG[L]
    uid = f"dn{L}"
    I = cfg["I"]
    ICN = I // P
    hmd = b.dram(f"hm{L}")
    wdTd = b.dram(f"wdT{L}")
    xin = b.dram("r1" if L == 0 else "r3")
    xout = b.dram("r2" if L == 0 else "out")

    pre = getattr(b, "_wdT_ready", {}).get(L)
    if pre is None:
        b.push_frame()
        tpb = TPB(b, uid)
        wst = wdT_state(b, L, tpb, uid)
        for ot in range(16):
            pre = wdT_tile(b, L, wst, wdTd)
        b.wait("sync", pre[0], pre[1])
        b.wait("scalar", pre[0], pre[1])
        b.pop_frame()
    else:
        b.wait("sync", pre[0], pre[1])
        b.wait("scalar", pre[0], pre[1])

    hmg = b.sbuf(f"hmg{uid}", [P, ICN * 1024], BF16)
    yg = b.sbuf(f"yg{uid}", [P, 8 * D], BF16)
    ynm = [b.sbuf(f"ynm{uid}_{i}", [P, D], F32) for i in range(2)]
    rwb = [b.sbuf(f"rw{uid}_{i}", [P, 512], BF16) for i in range(4)]
    xrow = [b.sbuf(f"xr{uid}_{i}", [P, D], F32) for i in range(2)]
    if getattr(b, "_lnjunk", None) is None:
        b._lnjunk = b.sbuf(f"j{uid}", [P, D], F32)
    st = b.sbuf(f"st{uid}", [P, 12], F32)
    psD = [b.psum(f"psD{uid}_{i}", [P, 512], F32) for i in range(8)]

    s_ldh = b.sem(f"ldh{uid}")
    s_ldh2 = b.sem(f"ldh2{uid}")
    s_ldw2 = b.sem(f"ldw2{uid}")
    s_ldw3 = b.sem(f"ldw3{uid}")
    s_ldxr = b.sem(f"ldxr{uid}")
    s_mmD = b.sem(f"mmD{uid}")
    s_evDr = b.sem(f"evDr{uid}")
    s_evDd = b.sem(f"evDd{uid}")
    s_std3 = b.sem(f"std3{uid}")
    s_nrm = b.sem(f"nrm{uid}")
    s_res = b.sem(f"res{uid}")
    s_sto = b.sem(f"sto{uid}")
    b.store_sems.add(s_sto)

    hview = hmg[:].rearrange("p (a t) -> p a t", t=1024)
    half = ICN // 2
    hm_gate = None
    rw_gate = [None] * 4
    yg_gate = None
    psgate = None
    stc_gate = [None, None]
    xr_gate = [None, None]
    tc = 0
    for g in range(4):
        if hm_gate is not None:
            b.wait("sync", hm_gate[0], hm_gate[1])
            b.wait("scalar", hm_gate[0], hm_gate[1])
        src = hmd[:, g * 1024:(g + 1) * 1024].rearrange(
            "(a p) t -> p a t", p=P)
        ldh = b.dma("sync", hview[:, 0:half, :], src[:, 0:half, :], s_ldh)
        ldh2 = b.dma("scalar", hview[:, half:ICN, :], src[:, half:ICN, :],
                     s_ldh2)
        b.wait("tensor", s_ldh, ldh)
        b.wait("tensor", s_ldh2, ldh2)
        for dc in range(4):
            if psgate is not None:
                b.wait("tensor", psgate[0], psgate[1])
            if dc == 0 and yg_gate is not None:
                b.wait("scalar", yg_gate[0], yg_gate[1])
            for ic in range(ICN):
                rb = tc % 4
                q = "sync" if tc % 2 == 0 else "scalar"
                qs = s_ldw2 if tc % 2 == 0 else s_ldw3
                tc += 1
                if rw_gate[rb] is not None:
                    b.wait(q, rw_gate[rb][0], rw_gate[rb][1])
                ldw = b.dma(q, rwb[rb][:],
                            wdTd[ic * P:(ic + 1) * P, dc * 512:(dc + 1) * 512],
                            qs)
                b.wait("tensor", qs, ldw)
                for j in range(8):
                    lhsT = hmg[:, ic * 1024 + j * P: ic * 1024 + (j + 1) * P]
                    fn = (lambda e, o=psD[j][:], l=lhsT, r=rwb[rb][:],
                          stf=(ic == 0), sp=(ic == ICN - 1):
                          e.matmul(o, l, r, start=stf, stop=sp))
                    if j == 7:
                        b.inc_on("tensor", fn, s_mmD, 1)
                    else:
                        b.op("tensor", fn)
                rw_gate[rb] = (s_mmD, s_mmD.n)
            b.waitn("scalar", s_mmD, s_mmD.n)
            for j in range(8):
                b.op("scalar", lambda e,
                     o=yg[:, j * D + dc * 512: j * D + (dc + 1) * 512],
                     i=psD[j][:]:
                     e.activation(o, i, AF.Copy, bias=0.0, scale=1.0))
            b.drain_inc("scalar", s_evDr, 8)
            psgate = (s_evDr, s_evDr.n)
        hm_gate = (s_mmD, s_mmD.n)
        evd_t = b.drain_inc("scalar", s_evDd, 1)
        for j in range(8):
            tt = 8 * g + j
            bi = j % 2
            stcol = st[:, bi * 6:bi * 6 + 6]
            if stc_gate[bi] is not None:
                b.wait("scalar", stc_gate[bi][0], stc_gate[bi][1])
                b.wait("vector", stc_gate[bi][0], stc_gate[bi][1])
            if xr_gate[bi] is not None:
                b.wait("sync", xr_gate[bi][0], xr_gate[bi][1])
            ldx_t = b.dma("sync", xrow[bi][:], xin[tt * P:(tt + 1) * P, :],
                          s_ldxr)
            std_t = ln_epilogue_stats(b, uid, yg[:, j * D:(j + 1) * D],
                                      stcol, s_evDd, evd_t, s_std3)
            b.wait("scalar", s_std3, std_t)
            b.op("scalar", lambda e, y=yg[:, j * D:(j + 1) * D],
                 o=ynm[bi][:], sc=stcol:
                 e.activation(o, y, AF.Identity, bias=sc[:, 3:4],
                              scale=sc[:, 5:6]))
            nrm_t = b.drain_inc("scalar", s_nrm, 1)
            b.wait("vector", s_nrm, nrm_t)
            b.wait("vector", s_ldxr, ldx_t)
            b.op("vector", lambda e, x=xrow[bi], y=ynm[bi]:
                 e.tensor_tensor(x[:], x[:], y[:], op=ALU.add))
            b.drain_inc("vector", s_res, 1)
            stc_gate[bi] = (s_res, s_res.n)
            b.wait("gpsimd", s_res, s_res.n)
            b.dma("gpsimd", xout[tt * P:(tt + 1) * P, :], xrow[bi][:], s_sto)
            xr_gate[bi] = (s_sto, s_sto.n)
        yg_gate = (s_res, s_res.n)



PHASES = [
    ("l0_lnqkv", lambda b: ph_lnqkv(b, 0),
     ["r0"], ["qf0", "kf0", "vt0"]),
    ("l0_attn", lambda b: ph_attn(b, 0),
     ["qf0", "kf0", "vt0"], ["zf0"]),
    ("l0_out", lambda b: ph_outproj(b, 0),
     ["zf0", "r0"], ["r1"]),
    ("l0_gateup", lambda b: ph_lngateup(b, 0),
     ["r1"], ["hm0"]),
    ("l0_down", lambda b: ph_down(b, 0),
     ["hm0", "r1"], ["r2", "wdT0"]),
    ("l1_lnqkv", lambda b: ph_lnqkv(b, 1),
     ["r2"], ["qf1", "kf1", "vt1"]),
    ("l1_attn", lambda b: ph_attn(b, 1),
     ["qf1", "kf1", "vt1"], ["zf1"]),
    ("l1_out", lambda b: ph_outproj(b, 1),
     ["zf1", "r2"], ["r3"]),
    ("l1_gateup", lambda b: ph_lngateup(b, 1),
     ["r3"], ["hm1"]),
    ("l1_down", lambda b: ph_down(b, 1),
     ["hm1", "r3"], ["out", "wdT1"]),
]


def build_phase_program(names):
    specs = [p for p in PHASES if p[0] in names]
    reads = set()
    writes = set()
    for (_, _, r, w) in specs:
        reads |= set(r)
        writes |= set(w)
    reads -= writes
    b = B("dev", reads=reads, writes=writes)
    b.store_sems = set()
    emit_consts(b)
    for (_, fn, _, _) in specs:
        fn(b)
    for s in sorted(b.store_sems, key=lambda s: s.name):
        b.wait("gpsimd", s, s.n)
    nc = b.finalize()
    return b, nc


def make_consts():
    ident = np.eye(P, dtype=NPBF)
    i = np.arange(P)
    maskC = np.where(i[None, :] <= i[:, None], 0.0, NEGBIG).astype(np.float32)
    maskA = np.where(i[:, None] <= i[None, :], 0.0, NEGBIG).astype(np.float32)
    half = 64
    inv = 1.0 / (10000.0 ** (np.arange(half, dtype=np.float64) / half))
    t = np.arange(T, dtype=np.float64)
    fr = np.outer(t, inv)
    cos = np.cos(fr).astype(np.float32)
    sin = np.sin(fr).astype(np.float32)
    ropeC = np.zeros((P, NT * 64), np.float32)
    ropeS = np.zeros((P, NT * 64), np.float32)
    for tt in range(NT):
        ropeC[:, tt * 64:(tt + 1) * 64] = cos[tt * P:(tt + 1) * P, :]
        ropeS[:, tt * 64:(tt + 1) * 64] = sin[tt * P:(tt + 1) * P, :]
    return dict(ident=ident, identf=np.eye(P, dtype=np.float32),
                maskC=maskC, maskA=maskA, ropeC=ropeC, ropeS=ropeS)


def build_full_program(upto=None, extra_out=(), extra_in=(), phases=None):
    b = B("full")
    b.store_sems = set()
    b._lnjunk = None
    saved = {}
    for n in extra_out:
        saved[n] = DRAM_SPEC[n]
        DRAM_SPEC[n] = (saved[n][0], saved[n][1], "output")
    for n in extra_in:
        saved[n] = DRAM_SPEC[n]
        DRAM_SPEC[n] = (saved[n][0], saved[n][1], "input")
    emit_consts(b)
    b._lnjunk = b.sbuf("lnjunk_shared", [P, D], F32)
    bar = b.sem("phasebar")
    plist = phases if phases is not None else PHASES[:upto]
    for i, (name, fn, reads, writes) in enumerate(plist):
        b.push_frame()
        fn(b)
        b.pop_frame()
        for s in sorted(b.store_sems, key=lambda s: s.name):
            b.wait("gpsimd", s, s.n)
        tgt = bar.n + 5
        for eng in ENGINES:
            if eng in ("sync", "gpsimd"):
                b.emit(eng, lambda e, h=bar.h: e.sem_inc(h, 1))
            else:
                b.emit(eng, lambda e, h=bar.h: e.drain().then_inc(h, 1))
        bar.inc(5)
        for eng in ENGINES:
            b.wait(eng, bar, tgt)
    nc = b.finalize()
    for n, v in saved.items():
        DRAM_SPEC[n] = v
    return b, nc


_PROG_CACHE = {}


def _get_prog(names):
    key = tuple(names)
    if key not in _PROG_CACHE:
        _PROG_CACHE[key] = build_phase_program(names)
    return _PROG_CACHE[key]


def kernel_multilaunch(**inputs):
    ncores = 8
    consts = make_consts()
    x = np.ascontiguousarray(inputs["L0_attn_x"])
    state = [dict(consts) for _ in range(ncores)]
    for c in range(ncores):
        state[c]["r0"] = np.ascontiguousarray(x[c])
        for k, v in inputs.items():
            if k != "L0_attn_x":
                state[c][k] = np.asarray(v)
    for (name, fn, reads, writes) in PHASES:
        b, nc = _get_prog([name])
        in_maps = [{n: state[c][n] for n in b.ext_in} for c in range(ncores)]
        res = run_bass_kernel_spmd(nc, in_maps, core_ids=list(range(ncores)))
        for c in range(ncores):
            for n in b.ext_out:
                state[c][n] = np.asarray(res.results[c][n])
    return np.stack([state[c]["out"] for c in range(ncores)], axis=0)


_FULL_CACHE = []


def kernel(**inputs):
    ncores = 8
    consts = make_consts()
    x = np.ascontiguousarray(inputs["L0_attn_x"])
    if not _FULL_CACHE:
        _FULL_CACHE.append(build_full_program())
    b, nc = _FULL_CACHE[0]
    in_maps = []
    for c in range(ncores):
        m = dict(consts)
        m["r0"] = np.ascontiguousarray(x[c])
        for k, v in inputs.items():
            if k != "L0_attn_x":
                m[k] = np.asarray(v)
        in_maps.append({n: m[n] for n in b.ext_in})
    res = run_bass_kernel_spmd(nc, in_maps, core_ids=list(range(ncores)))
    return np.stack([np.asarray(res.results[c]["out"]) for c in range(ncores)],
                    axis=0)


# version 19
# speedup vs baseline: 1.2952x; 1.2952x over previous
import contextlib
import math
import numpy as np
import ml_dtypes
import concourse.bass as bass
import concourse.mybir as mybir
from concourse.bass_utils import run_bass_kernel_spmd

F32 = mybir.dt.float32
BF16 = mybir.dt.bfloat16
F16 = mybir.dt.float16
I32 = mybir.dt.int32
U8 = mybir.dt.uint8
AF = mybir.ActivationFunctionType
ALU = mybir.AluOpType
AX = mybir.AxisListType
NPBF = ml_dtypes.bfloat16

P = 128
T = 4096
D = 2048
NT = T // P
ND = D // P
RSQ_HD = 1.0 / math.sqrt(128.0)
NEGBIG = -1e30
EPS_LN = 1e-5

CFG = {
    0: dict(OQ=2048, OK=2048, OV=2048, NH=16, KVH=16, rope=False, window=None,
            I=5632,
            wq="L0_attn_Wq", wk="L0_attn_Wk", wv="L0_attn_Wv", wo="L0_attn_Wo"),
    1: dict(OQ=2048, OK=512, OV=512, NH=16, KVH=4, rope=True, window=512,
            I=8192,
            wq="L1_attn_wq", wk="L1_attn_wk", wv="L1_attn_wv", wo="L1_attn_wo"),
}

DRAM_SPEC = {
    "r0": ([T, D], F32, "input"),
    "r1": ([T, D], F32, "inter"),
    "r2": ([T, D], F32, "inter"),
    "r3": ([T, D], F32, "inter"),
    "out": ([T, D], F32, "output"),
    "ident": ([P, P], BF16, "input"),
    "identf": ([P, P], F32, "input"),
    "maskC": ([P, P], F32, "input"),
    "maskA": ([P, P], F32, "input"),
    "ropeC": ([P, NT * 64], F32, "input"),
    "ropeS": ([P, NT * 64], F32, "input"),
    "L0_attn_Wq": ([2048, 2048], F32, "input"),
    "L0_attn_Wk": ([2048, 2048], F32, "input"),
    "L0_attn_Wv": ([2048, 2048], F32, "input"),
    "L0_attn_Wo": ([2048, 2048], F32, "input"),
    "L0_ffn_gate_up_packed": ([11264, 256], I32, "input"),
    "L0_ffn_gate_up_scales": ([16, 11264], F16, "input"),
    "L0_ffn_down_packed": ([2048, 704], I32, "input"),
    "L0_ffn_down_scales": ([44, 2048], F16, "input"),
    "L1_attn_wq": ([2048, 2048], F32, "input"),
    "L1_attn_wk": ([512, 2048], F32, "input"),
    "L1_attn_wv": ([512, 2048], F32, "input"),
    "L1_attn_wo": ([2048, 2048], F32, "input"),
    "L1_ffn_w_gate": ([8192, 2048], F32, "input"),
    "L1_ffn_w_up": ([8192, 2048], F32, "input"),
    "L1_ffn_w_down": ([2048, 8192], F32, "input"),
    "qf0": ([2048, T], BF16, "inter"),
    "kf0": ([2048, T], BF16, "inter"),
    "vt0": ([T, 2048], BF16, "inter"),
    "zf0": ([2048, T], BF16, "inter"),
    "hm0": ([5632, T], BF16, "inter"),
    "wdT0": ([5632, 2048], BF16, "inter"),
    "qf1": ([2048, T], BF16, "inter"),
    "kf1": ([512, T], BF16, "inter"),
    "vt1": ([T, 512], BF16, "inter"),
    "zf1": ([2048, T], BF16, "inter"),
    "hm1": ([8192, T], BF16, "inter"),
    "wdT1": ([8192, 2048], BF16, "inter"),
}

ENGINES = ("sync", "scalar", "vector", "tensor", "gpsimd")


class Sem:
    __slots__ = ("h", "n", "name")

    def __init__(self, handle, name):
        self.h = handle
        self.n = 0
        self.name = name

    def inc(self, k=1):
        self.n += k
        return self.n


class B:

    def __init__(self, mode, reads=(), writes=()):
        self.nc = bass.Bass("TRN2", target_bir_lowering=False)
        self.mode = mode
        self.reads = set(reads)
        self.writes = set(writes)
        self.ops = {e: [] for e in ENGINES}
        self.sems = {}
        self.dram_handles = {}
        self.stack = contextlib.ExitStack()
        self.ext_in = []
        self.ext_out = []

    _UIDS = ("qk0", "qk1", "at0", "at1", "op0", "op1", "gu0", "gu1",
             "dn0", "dn1")

    def sem(self, name):
        if self.mode == "full":
            for u in self._UIDS:
                if name.endswith(u):
                    name = name[:-len(u)] + "X"
                    break
        if name not in self.sems:
            h = self.stack.enter_context(self.nc.semaphore(name))
            self.sems[name] = Sem(h, name)
        return self.sems[name]

    def push_frame(self):
        self._frame = contextlib.ExitStack()

    def pop_frame(self):
        self._frame.close()
        self._frame = None

    def sbuf(self, name, shape, dtype):
        stk = getattr(self, "_frame", None) or self.stack
        return stk.enter_context(self.nc.sbuf_tensor(name, shape, dtype))

    def psum(self, name, shape, dtype):
        stk = getattr(self, "_frame", None) or self.stack
        return stk.enter_context(self.nc.psum_tensor(name, shape, dtype))

    def dram(self, name):
        if name in self.dram_handles:
            return self.dram_handles[name]
        shape, dtype, kind = DRAM_SPEC[name]
        if kind == "input":
            k = "ExternalInput"
            self.ext_in.append(name)
        elif kind == "output":
            k = "ExternalOutput"
            self.ext_out.append(name)
        else:
            if self.mode == "dev":
                if name in self.writes:
                    k = "ExternalOutput"
                    self.ext_out.append(name)
                else:
                    k = "ExternalInput"
                    self.ext_in.append(name)
            else:
                k = "Internal"
        h = self.nc.dram_tensor(name, shape, dtype, kind=k)
        self.dram_handles[name] = h
        return h

    def emit(self, eng, fn):
        self.ops[eng].append(fn)

    def dma(self, eng, out, in_, sem, **kw):
        tgt = sem.inc(16)
        self.emit(eng, lambda e, o=out, i=in_, s=sem.h, kw=kw:
                  e.dma_start(out=o, in_=i, **kw).then_inc(s, 16))
        return tgt

    def wait(self, eng, sem, val):
        self.emit(eng, lambda e, s=sem.h, v=val: e.wait_ge(s, v))

    def waitn(self, eng, sem, val, cyc=384):
        self.emit(eng, lambda e, s=sem.h, v=val: e.wait_ge(s, v))
        self.emit(eng, lambda e, c=cyc: e.nop(cycle_cnt=c))

    def inc_on(self, eng, fn, sem, k=1):
        tgt = sem.inc(k)
        self.emit(eng, lambda e, f=fn, s=sem.h, kk=k: f(e).then_inc(s, kk))
        return tgt

    def op(self, eng, fn):
        self.emit(eng, fn)

    def drain_inc(self, eng, sem, k=1):
        tgt = sem.inc(k)
        self.emit(eng, lambda e, s=sem.h, kk=k: e.drain().then_inc(s, kk))
        return tgt

    def finalize(self):
        with self.nc.Block() as blk:
            for eng in ENGINES:
                def mk(engname):
                    def body(e):
                        for fn in self.ops[engname]:
                            fn(e)
                    return body
                getattr(blk, eng)(mk(eng))
        return self.nc



def emit_consts(b):
    b.identsb = b.sbuf("identsb", [P, P], BF16)
    b.identf = b.sbuf("identfsb", [P, P], F32)
    b.ones1 = b.sbuf("ones1", [1, P], F32)
    b.epsv = b.sbuf("epsv", [P, 1], F32)
    b.m16 = b.sbuf("m16c", [P, 1], F32)
    b.onescol = b.sbuf("onescol", [P, 1], BF16)
    s = b.sem("ldc")
    b.dma("sync", b.identsb[:], b.dram("ident")[:], s)
    b.dma("sync", b.identf[:], b.dram("identf")[:], s)
    cst = b.sem("cst")
    b.wait("vector", s, s.n)
    b.op("vector", lambda e: e.memset(b.ones1[:], 1.0))
    b.op("vector", lambda e: e.memset(b.epsv[:], EPS_LN))
    b.op("vector", lambda e: e.memset(b.m16[:], -16.0))
    b.op("vector", lambda e: e.memset(b.onescol[:], 1.0))
    done = b.drain_inc("vector", cst, 1)
    for eng in ("scalar", "tensor", "gpsimd", "sync"):
        b.wait(eng, cst, done)


def ln_epilogue_stats(b, uid, src_f32, stcol, s_in_ready, in_tgt, s_done):
    junk = b._lnjunk
    b.wait("scalar", s_in_ready, in_tgt)
    b.op("scalar", lambda e: e.activation(junk[:], src_f32, AF.Square,
                                          bias=0.0, scale=1.0,
                                          accum_out=stcol[:, 1:2]))
    ssq_t = b.drain_inc("scalar", b.sem(f"lnsq{uid}"), 1)
    b.wait("vector", s_in_ready, in_tgt)
    b.op("vector", lambda e: e.reduce_sum(stcol[:, 0:1], src_f32, axis=AX.X))
    vaux = b.sem(f"lnvaux{uid}")
    va = b.drain_inc("vector", vaux, 1)
    b.wait("vector", vaux, va)
    b.wait("vector", b.sem(f"lnsq{uid}"), ssq_t)

    def stats1(e):
        e.tensor_scalar(stcol[:, 2:3], stcol[:, 0:1], 1.0 / D, None,
                        op0=ALU.mult)
        e.tensor_scalar(stcol[:, 3:4], stcol[:, 0:1], -1.0 / D, None,
                        op0=ALU.mult)
        e.tensor_scalar(stcol[:, 4:5], stcol[:, 1:2], 1.0 / D, None,
                        op0=ALU.mult)
    b.op("vector", stats1)
    va = b.drain_inc("vector", vaux, 1)
    b.wait("vector", vaux, va)
    b.op("vector", lambda e: e.tensor_tensor(
        stcol[:, 5:6], stcol[:, 2:3], stcol[:, 2:3], op=ALU.mult))
    va = b.drain_inc("vector", vaux, 1)
    b.wait("vector", vaux, va)
    b.op("vector", lambda e: e.scalar_tensor_tensor(
        stcol[:, 4:5], stcol[:, 5:6], -1.0, stcol[:, 4:5],
        op0=ALU.mult, op1=ALU.add))
    vst_t = b.drain_inc("vector", b.sem(f"lnvst{uid}"), 1)
    b.wait("scalar", b.sem(f"lnvst{uid}"), vst_t)

    def rstd_chain(e):
        var = stcol[:, 4:5]
        rstd = stcol[:, 5:6]
        e.activation(rstd, var, AF.Ln, bias=b.epsv[:], scale=1.0)
        e.activation(rstd, rstd, AF.Exp, bias=0.0, scale=-0.5)
    b.op("scalar", rstd_chain)
    aux = b.sem(f"lnaux{uid}")
    aux_t = b.drain_inc("scalar", aux, 1)
    b.wait("scalar", aux, aux_t)
    b.op("scalar", lambda e: e.activation(
        stcol[:, 3:4], stcol[:, 3:4], AF.Identity, bias=0.0,
        scale=stcol[:, 5:6]))
    return b.drain_inc("scalar", s_done, 1)


def emit_ln_to_hfm(b, L, xname, hfm, uid, tpb=None):
    x = b.dram(xname)
    xa = [b.sbuf(f"lnx{uid}_0", [P, D], F32)] * 2
    xnb = [b.sbuf(f"lnxn{uid}_{i}", [P, D], BF16) for i in range(2)]
    if getattr(b, "_lnjunk", None) is None:
        b._lnjunk = b.sbuf(f"lnj{uid}", [P, D], F32)
    st = b.sbuf(f"lnst{uid}", [P, 12], F32)
    if tpb is None:
        tpb = TPB(b, uid)
    s_ldx = b.sem(f"ldx{uid}")
    s_xn = b.sem(f"xn{uid}")
    s_tr = tpb.s_tr
    s_evh = tpb.s_ev
    s_std = b.sem(f"std{uid}")
    ldx0, xn0, tr0, evh0, std0 = s_ldx.n, s_xn.n, s_tr.n, s_evh.n, s_std.n
    hview = hfm[:].rearrange("p (c t) -> p c t", t=T)

    for tt in range(NT):
        bi = tt % 2
        stcol = st[:, bi * 6:bi * 6 + 6]
        if tt >= 1:
            b.wait("sync", s_xn, xn0 + tt)
        ldt = b.dma("sync", xa[bi][:], x[tt * P:(tt + 1) * P, :], s_ldx)
        if tt >= 2:
            b.wait("scalar", s_xn, xn0 + tt - 1)
            b.wait("vector", s_xn, xn0 + tt - 1)
        std_t = ln_epilogue_stats(b, uid, xa[bi][:], stcol, s_ldx, ldt, s_std)
        b.wait("scalar", s_std, std_t)
        if tt >= 2:
            b.wait("scalar", s_tr, tr0 + (tt - 1) * 2)
        b.op("scalar", lambda e, xb=xa[bi], ob=xnb[bi], sc=stcol:
             e.activation(ob[:], xb[:], AF.Identity, bias=sc[:, 3:4],
                          scale=sc[:, 5:6]))
        b.drain_inc("scalar", s_xn, 1)
        b.wait("tensor", s_xn, xn0 + tt + 1)
        for g0 in (0, 8):
            srcs = [xnb[bi][:, (g0 + i) * P:(g0 + i + 1) * P]
                    for i in range(8)]
            dstb = hview[:, g0:g0 + 8, tt * P:(tt + 1) * P]
            tpb.group(srcs, dstb)
    return s_evh, evh0 + NT * 2


class TPB:

    def __init__(self, b, uid):
        self.b = b
        self.banks = [b.psum(f"tpb{uid}_{i}", [P, 8 * P], BF16)
                      for i in range(2)]
        self.s_tr = b.sem(f"tpbtr{uid}")
        self.s_ev = b.sem(f"tpbev{uid}")
        self.g = 0

    def group(self, srcs, dst_batched, gate=None, ev_eng="vector",
              ev_scale=None):
        b = self.b
        n = len(srcs)
        bank = self.banks[self.g % 2]
        if self.g >= 2:
            b.wait("tensor", self.s_ev, self.s_ev.n - 1)
        for i, s in enumerate(srcs):
            fn = (lambda e, o=bank[:, i * P:(i + 1) * P], i_=s:
                  e.transpose(o, i_, b.identsb[:]))
            if i == n - 1:
                b.inc_on("tensor", fn, self.s_tr, 1)
            else:
                b.op("tensor", fn)
        b.waitn(ev_eng, self.s_tr, self.s_tr.n)
        if gate is not None:
            b.wait(ev_eng, gate[0], gate[1])
        if ev_eng == "vector":
            b.op("vector", lambda e, o=dst_batched, i=bank[:, 0:n * P]:
                 e.tensor_copy(o, i))
        else:
            sc = 1.0 if ev_scale is None else ev_scale
            b.op("scalar", lambda e, o=dst_batched, i=bank[:, 0:n * P],
                 s=sc: e.activation(o, i, AF.Copy, bias=0.0, scale=s))
        tgt = b.drain_inc(ev_eng, self.s_ev, 1)
        self.g += 1
        return self.s_ev, tgt


class WT:

    def __init__(self, b, uid, tpb, ncols=D):
        self.b = b
        self.uid = uid
        self.tpb = tpb
        self.ncols = ncols
        self.ndc = ncols // P
        self.wsb = [b.sbuf(f"wt{uid}_{i}", [P, ncols], BF16) for i in range(2)]
        self.s_ldw = b.sem(f"wtld{uid}")
        self.s_evw = tpb.s_ev
        self.ctr = 0

    def prep(self, wname, row0, nrows, dst, dst_off, ostride,
             gate=None):
        b = self.b
        W = b.dram(wname)
        NDC = self.ndc
        nj = nrows // P
        ngroups_per_j = (NDC + 7) // 8
        for j in range(nj):
            bi = self.ctr % 2
            if self.ctr >= 2:
                b.wait("gpsimd", self.tpb.s_tr,
                       self.tpb.s_tr.n - ngroups_per_j)
            ldt = b.dma("gpsimd", self.wsb[bi][:],
                        W[row0 + j * P: row0 + (j + 1) * P, :], self.s_ldw)
            b.wait("tensor", self.s_ldw, ldt)
            for g0 in range(0, NDC, 8):
                n = min(8, NDC - g0)
                srcs = [self.wsb[bi][:, (g0 + i) * P:(g0 + i + 1) * P]
                        for i in range(n)]
                r0 = dst_off % ostride + j * P
                if ostride == P:
                    off = dst_off + g0 * ostride + j * P
                    dstb = dst[:, off:off + n * P]
                else:
                    c0 = g0 + dst_off // ostride
                    dstb = dst[:].rearrange("p (c q) -> p c q",
                                            q=ostride)[:, c0:c0 + n,
                                                       r0:r0 + P]
                self.tpb.group(srcs, dstb, gate=gate)
                gate = None
            self.ctr += 1
        return self.s_evw, self.s_evw.n



def ph_lnqkv(b, L):
    cfg = CFG[L]
    uid = f"qk{L}"
    hfm = b.sbuf(f"hfm{uid}", [P, ND * T], BF16)
    tpb = TPB(b, uid)
    s_evh, evh_tgt = emit_ln_to_hfm(b, L, "r0" if L == 0 else "r2", hfm, uid,
                                    tpb=tpb)
    b.wait("tensor", s_evh, evh_tgt)

    wt = WT(b, uid, tpb)
    s_mm = b.sem(f"mm{uid}")
    s_ev = b.sem(f"ev{uid}")
    s_st = b.sem(f"st{uid}")
    b.store_sems.add(s_st)

    if L == 0:
        wTb = [b.sbuf(f"wT{uid}_{i}", [P, ND * P], BF16) for i in range(2)]
        psq = [b.psum(f"psq{uid}_{i}", [P, 512], F32) for i in range(4)]
        qst = [b.sbuf(f"qst{uid}_{i}", [P, 512], BF16) for i in range(2)]
        ev_ctr = 0
        for (wname, outname, scale) in ((cfg["wq"], "qf0", RSQ_HD),
                                        (cfg["wk"], "kf0", 1.0)):
            outd = b.dram(outname)
            for oc in range(16):
                pb = oc % 2
                gate = (s_mm, s_mm.n) if wt.ctr >= 2 else None
                _, evw_tgt = wt.prep(wname, oc * P, P, wTb[pb], 0, P, gate=gate)
                b.wait("tensor", wt.s_evw, evw_tgt)
                for tg in range(8):
                    bank = psq[ev_ctr % 4]
                    if ev_ctr >= 4:
                        b.wait("tensor", s_ev, s_ev.n - 3)
                    for dc in range(ND):
                        rhs = hfm[:, dc * T + tg * 512:
                                  dc * T + (tg + 1) * 512]
                        lhsT = wTb[pb][:, dc * P:(dc + 1) * P]
                        fn = (lambda e, o=bank[:], l=lhsT, r=rhs,
                              st=(dc == 0), sp=(dc == ND - 1):
                              e.matmul(o, l, r, start=st, stop=sp))
                        if dc == ND - 1:
                            b.inc_on("tensor", fn, s_mm, 1)
                        else:
                            b.op("tensor", fn)
                    b.waitn("scalar", s_mm, s_mm.n)
                    sb = ev_ctr % 2
                    if ev_ctr >= 2:
                        b.wait("scalar", s_st, s_st.n - 16)
                    b.op("scalar", lambda e, o=qst[sb][:], i=bank[:],
                         sc=scale: e.activation(o, i, AF.Copy, bias=0.0,
                                                scale=sc))
                    b.drain_inc("scalar", s_ev, 1)
                    b.wait("gpsimd", s_ev, s_ev.n)
                    b.dma("gpsimd", outd[oc * P:(oc + 1) * P,
                                         tg * 512:(tg + 1) * 512],
                          qst[sb][:], s_st)
                    ev_ctr += 1
        wvTs = b.sbuf(f"wvT{uid}", [P, ND * 512], BF16)
        psv = [b.psum(f"psv{uid}_{i}", [P, 512], F32) for i in range(2)]
        vst = [b.sbuf(f"vst{uid}_{i}", [P, 512], BF16) for i in range(2)]
        outd = b.dram("vt0")
        s_mmv = b.sem(f"mmv{uid}")
        s_evv = b.sem(f"evv{uid}")
        s_stv = b.sem(f"stv{uid}")
        b.store_sems.add(s_stv)
        for nc4 in range(4):
            gate = (s_mmv, s_mmv.n) if nc4 > 0 else None
            _, evw_tgt = wt.prep(cfg["wv"], nc4 * 512, 512, wvTs, 0, 512,
                                 gate=gate)
            b.wait("tensor", wt.s_evw, evw_tgt)
            for tt in range(NT):
                pb = tt % 2
                if tt >= 2 or nc4 > 0:
                    b.wait("tensor", s_evv, max(0, s_evv.n - 1))
                for dc in range(ND):
                    lhsT = hfm[:, dc * T + tt * P: dc * T + (tt + 1) * P]
                    rhs = wvTs[:, dc * 512:(dc + 1) * 512]
                    fn = (lambda e, o=psv[pb][:], l=lhsT, r=rhs,
                          st=(dc == 0), sp=(dc == ND - 1):
                          e.matmul(o, l, r, start=st, stop=sp))
                    if dc == ND - 1:
                        b.inc_on("tensor", fn, s_mmv, 1)
                    else:
                        b.op("tensor", fn)
                b.waitn("scalar", s_mmv, s_mmv.n)
                if tt >= 2 or nc4 > 0:
                    b.wait("scalar", s_stv, max(0, s_stv.n - 16))
                b.op("scalar", lambda e, o=vst[pb][:], i=psv[pb][:]:
                     e.activation(o, i, AF.Copy, bias=0.0, scale=1.0))
                b.drain_inc("scalar", s_evv, 1)
                b.wait("gpsimd", s_evv, s_evv.n)
                b.dma("gpsimd", outd[tt * P:(tt + 1) * P,
                                     nc4 * 512:(nc4 + 1) * 512],
                      vst[pb][:], s_stv)
    else:
        ropeC = b.sbuf(f"ropeC{uid}", [P, NT * 64], F32)
        ropeS = b.sbuf(f"ropeS{uid}", [P, NT * 64], F32)
        s_ldc2 = b.sem(f"ldc2{uid}")
        b.dma("sync", ropeC[:], b.dram("ropeC")[:], s_ldc2)
        b.dma("sync", ropeS[:], b.dram("ropeS")[:], s_ldc2)
        b.wait("vector", s_ldc2, s_ldc2.n)

        wTs = b.sbuf(f"wTs{uid}", [P, ND * 512], BF16)
        psp = [b.psum(f"psp{uid}_{i}", [P, 512], F32) for i in range(2)]
        rtmp = b._lnjunk
        ropebf = [b.sbuf(f"rbf{uid}_{i}", [P, 512], BF16) for i in range(2)]
        hstage = [b.sbuf(f"hst{uid}_{i}", [P, 4 * P], BF16) for i in range(2)]
        vstage = ropebf
        s_mmp = b.sem(f"mmp{uid}")
        s_rp = b.sem(f"rp{uid}")
        s_trh = b.sem(f"trh{uid}")
        s_evh2 = b.sem(f"evh2{uid}")
        s_sth = b.sem(f"sth{uid}")
        b.store_sems.add(s_sth)
        tr_ctr = 0
        bank_gate = [None, None]
        for (wname, O, outname, isq) in ((cfg["wq"], 2048, "qf1", True),
                                         (cfg["wk"], 512, "kf1", True),
                                         (cfg["wv"], 512, "vt1", False)):
            outd = b.dram(outname)
            for nc4 in range(O // 512):
                gate = ((s_mmp, s_mmp.n) if s_mmp.n > 0 else None)
                _, evw_tgt = wt.prep(wname, nc4 * 512, 512, wTs, 0, 512,
                                     gate=gate)
                b.wait("tensor", wt.s_evw, evw_tgt)
                for tt in range(NT):
                    pb = tt % 2
                    if bank_gate[pb] is not None:
                        b.wait("tensor", bank_gate[pb][0], bank_gate[pb][1])
                    for dc in range(ND):
                        lhsT = hfm[:, dc * T + tt * P: dc * T + (tt + 1) * P]
                        rhs = wTs[:, dc * 512:(dc + 1) * 512]
                        fn = (lambda e, o=psp[pb][:], l=lhsT, r=rhs,
                              st=(dc == 0), sp=(dc == ND - 1):
                              e.matmul(o, l, r, start=st, stop=sp))
                        if dc == ND - 1:
                            b.inc_on("tensor", fn, s_mmp, 1)
                        else:
                            b.op("tensor", fn)
                    if isq:
                        b.waitn("vector", s_mmp, s_mmp.n)
                        def rope1(e, ps=psp[pb], tt=tt):
                            cs = ropeC[:, tt * 64:(tt + 1) * 64]
                            sn = ropeS[:, tt * 64:(tt + 1) * 64]
                            for j in range(4):
                                q1 = ps[:, j * P:j * P + 64]
                                q2 = ps[:, j * P + 64:(j + 1) * P]
                                o = rtmp[:, j * 256:(j + 1) * 256]
                                e.tensor_tensor(o[:, 0:64], q1, cs, op=ALU.mult)
                                e.tensor_tensor(o[:, 64:128], q2, sn, op=ALU.mult)
                                e.tensor_tensor(o[:, 128:192], q1, sn, op=ALU.mult)
                                e.tensor_tensor(o[:, 192:256], q2, cs, op=ALU.mult)
                        b.op("vector", rope1)
                        rt = b.drain_inc("vector", s_rp, 1)
                        b.wait("vector", s_rp, rt)
                        def rope2(e, ob=ropebf[pb]):
                            for j in range(4):
                                o = rtmp[:, j * 256:(j + 1) * 256]
                                e.tensor_tensor(ob[:, j * P:j * P + 64],
                                                o[:, 0:64], o[:, 64:128],
                                                op=ALU.subtract)
                                e.tensor_tensor(ob[:, j * P + 64:(j + 1) * P],
                                                o[:, 128:192], o[:, 192:256],
                                                op=ALU.add)
                        b.op("vector", rope2)
                        b.drain_inc("vector", s_rp, 1)
                        bank_gate[pb] = (s_rp, s_rp.n)
                        b.wait("tensor", s_rp, s_rp.n)
                        sb = tr_ctr % 2
                        gate_h = ((s_sth, s_sth.n - 16)
                                  if tr_ctr >= 2 else None)
                        srcs = [ropebf[pb][:, j * P:(j + 1) * P]
                                for j in range(4)]
                        _, ev2 = tpb.group(
                            srcs, hstage[sb][:], gate=gate_h,
                            ev_eng="scalar",
                            ev_scale=(RSQ_HD if wname == cfg["wq"] else 1.0))
                        b.wait("gpsimd", tpb.s_ev, ev2)
                        b.dma("gpsimd",
                              outd[nc4 * 4 * P:(nc4 * 4 + 4) * P,
                                   tt * P:(tt + 1) * P].rearrange(
                                       "(j p) t -> p j t", p=P),
                              hstage[sb][:].rearrange("p (j t) -> p j t", t=P),
                              s_sth)
                        tr_ctr += 1
                    else:
                        b.waitn("scalar", s_mmp, s_mmp.n)
                        sb = tt % 2
                        if tt >= 2:
                            b.wait("scalar", s_sth, s_sth.n - 16)
                        b.op("scalar", lambda e, o=vstage[sb][:], i=psp[pb][:]:
                             e.activation(o, i, AF.Copy, bias=0.0, scale=1.0))
                        b.drain_inc("scalar", s_evh2, 1)
                        bank_gate[pb] = (s_evh2, s_evh2.n)
                        b.wait("gpsimd", s_evh2, s_evh2.n)
                        b.dma("gpsimd", outd[tt * P:(tt + 1) * P, 0:512],
                              vstage[sb][:], s_sth)



def wdT_state(b, L, tpb, uid):
    st = {}
    st["s_stw"] = b.sem(f"stw{uid}")
    b.store_sems.add(st["s_stw"])
    if L == 0:
        st["dq"] = DQ(b, uid, "L0_ffn_down_packed", "L0_ffn_down_scales",
                      5632, 44, tpb=tpb)
        st["scTd"] = b.sbuf(f"scTd{uid}", [P, 16 * 44], F32)
        st["dq"].prep_scales(st["scTd"], 16)
        st["icn"] = 44
    else:
        st["wt"] = WT(b, uid, tpb, ncols=8192)
        st["icn"] = 64
    st["slab"] = b.sbuf(f"slab{uid}", [P, st["icn"] * P], BF16)
    st["n"] = 0
    return st


def wdT_tile(b, L, st, wdTd):
    i = st["n"]
    st["n"] += 1
    s_stw = st["s_stw"]
    gate = (s_stw, s_stw.n) if i >= 1 else None
    if L == 0:
        _, ev = st["dq"].prep(i * P, st["scTd"], i * 44, st["slab"], 0, P,
                              gate=gate)
        evsem = st["dq"].tpb.s_ev
    else:
        _, ev = st["wt"].prep("L1_ffn_w_down", i * P, P, st["slab"], 0, P,
                              gate=gate)
        evsem = st["wt"].s_evw
    b.wait("gpsimd", evsem, ev)
    for ic in range(st["icn"]):
        b.dma("gpsimd", wdTd[ic * P:(ic + 1) * P, i * P:(i + 1) * P],
              st["slab"][:, ic * P:(ic + 1) * P], s_stw)
    return (s_stw, s_stw.n)


def ph_attn(b, L):
    cfg = CFG[L]
    uid = f"at{L}"
    NH = cfg["NH"]
    RPG = NH // cfg["KVH"]
    window = cfg["window"]
    qf = b.dram(f"qf{L}")
    kf = b.dram(f"kf{L}")
    vt = b.dram(f"vt{L}")
    zf = b.dram(f"zf{L}")

    maskC = b.sbuf(f"mc{uid}", [P, P], F32)
    maskA = b.sbuf(f"ma{uid}", [P, P], F32)
    s_ldm = b.sem(f"ldm{uid}")
    b.dma("sync", maskC[:], b.dram("maskC")[:], s_ldm)
    b.dma("sync", maskA[:], b.dram("maskA")[:], s_ldm)
    mask_tgt = s_ldm.n
    b.wait("vector", s_ldm, mask_tgt)

    Qs = [b.sbuf(f"Qs{uid}_{i}", [P, T], BF16) for i in range(2)]
    Ks = [b.sbuf(f"Ks{uid}_{i}", [P, T], BF16) for i in range(2)]
    Vs = [b.sbuf(f"Vs{uid}_{i}", [P, T], BF16) for i in range(2)]
    PTs = [b.sbuf(f"PT{uid}_{i}", [P, 512], BF16) for i in range(4)]
    denrowb = [b.sbuf(f"dr{uid}_{i}", [1, 512], F32) for i in range(2)]
    dn4b = [b.sbuf(f"dn4{uid}_{i}", [P, 4], F32) for i in range(2)]
    rec4b = [b.sbuf(f"rc4{uid}_{i}", [P, 4], F32) for i in range(2)]
    rrowb = [b.sbuf(f"rr{uid}_{i}", [1, 512], F32) for i in range(2)]
    rbstb = [b.sbuf(f"rb{uid}_{i}", [P, 512], F32) for i in range(2)]
    zst = [b.sbuf(f"zs{uid}_{i}", [P, 512], BF16) for i in range(2)]

    psS = [b.psum(f"psS{uid}_{i}", [P, 512], F32) for i in range(3)]
    psZ = [b.psum(f"psZ{uid}_{i}", [P, 512], F32) for i in range(2)]
    psRb = [b.psum(f"psR{uid}_0", [P, 512], F32)] * 2
    if not hasattr(b, "dbg"):
        b.dbg = {}
    b.dbg.update({f"denrow{uid}": denrowb[0], f"dn4{uid}": dn4b[0],
                  f"rec4{uid}": rec4b[0], f"rrow{uid}": rrowb[0],
                  f"rbst{uid}": rbstb[0]})

    tpbA = TPB(b, uid)
    wdTd = b.dram(f"wdT{L}")
    wst = wdT_state(b, L, tpbA, uid)
    s_ldq = b.sem(f"ldq{uid}")
    s_mmS = b.sem(f"mmS{uid}")
    s_msk = b.sem(f"msk{uid}")
    s_exp = b.sem(f"exp{uid}")
    s_mmZ = b.sem(f"mmZ{uid}")
    s_dev = b.sem(f"dev{uid}")
    s_tr2 = b.sem(f"tr2{uid}")
    s_rec = b.sem(f"rec{uid}")
    s_bc = b.sem(f"bc{uid}")
    s_evrb = b.sem(f"evrb{uid}")
    s_z = b.sem(f"z{uid}")
    s_stz = b.sem(f"stz{uid}")
    b.store_sems.add(s_stz)

    qgate = [None, None]
    kgate = [None, None]
    vgate = [None, None]
    psS_gate = [None, None, None]
    psZ_gate = [None, None]
    pts_gate = [None, None, None, None]
    psR_gate = None
    kvloaded = -1
    stctr = 0

    for h in range(NH):
        bq = h % 2
        if qgate[bq] is not None:
            b.wait("sync", qgate[bq][0], qgate[bq][1])
        b.dma("sync", Qs[bq][:], qf[h * P:(h + 1) * P, :], s_ldq)
        kv = h // RPG
        bkv = kv % 2
        if kv != kvloaded:
            if kgate[bkv] is not None:
                b.wait("sync", kgate[bkv][0], kgate[bkv][1])
            b.dma("sync", Ks[bkv][:], kf[kv * P:(kv + 1) * P, :], s_ldq)
            if vgate[bkv] is not None:
                b.wait("sync", vgate[bkv][0], vgate[bkv][1])
            b.dma("sync",
                  Vs[bkv][:].rearrange("p (n o) -> p n o", o=P),
                  vt[:, kv * P:(kv + 1) * P].rearrange("(n p) o -> p n o", p=P),
                  s_ldq)
            kvloaded = kv
        ld_tgt = s_ldq.n
        b.wait("tensor", s_ldq, ld_tgt)

        for g in range(8):
            zb = g % 2
            psR = psRb[zb]
            denrow, dn4, rec4, rrow, rbst = (denrowb[zb], dn4b[zb],
                                             rec4b[zb], rrowb[zb], rbstb[zb])
            if psZ_gate[zb] is not None:
                b.wait("tensor", psZ_gate[zb][0], psZ_gate[zb][1])
            stlo = 0 if not window else max(0, 4 * g - 4)
            sthi = 4 * g + 4
            first = True
            sts = [st for st in range(stlo, sthi)
                   if max(0, st - 4 * g) <= (
                       3 if not window else min(3, st + 4 - 4 * g))]

            def flush(pend, exp_tgt, first):
                b.wait("tensor", s_exp, exp_tgt)
                for (pst, pj0, pj1, psb) in pend:
                    if first and psR_gate is not None:
                        b.wait("tensor", psR_gate[0], psR_gate[1])
                    b.op("tensor",
                         lambda e, o=psR[0:1, pj0 * P:(pj1 + 1) * P],
                         l=b.onescol[:],
                         r=PTs[psb][:, pj0 * P:(pj1 + 1) * P],
                         stf=first, sp=(pst == sts[-1]):
                         e.matmul(o, l, r, start=stf, stop=sp))
                    b.inc_on("tensor",
                             lambda e, o=psZ[zb][:, pj0 * P:(pj1 + 1) * P],
                             l=Vs[bkv][:, pst * P:(pst + 1) * P],
                             r=PTs[psb][:, pj0 * P:(pj1 + 1) * P],
                             stf=first, sp=(pst == sts[-1]):
                             e.matmul(o, l, r, start=stf, stop=sp),
                             s_mmZ, 1)
                    pts_gate[psb] = (s_mmZ, s_mmZ.n)
                    first = False
                return first

            pend = []
            prev = None
            for sti, st in enumerate(sts):
                j0 = max(0, st - 4 * g)
                j1 = 3 if not window else min(3, st + 4 - 4 * g)
                pb = stctr % 3
                sb = stctr % 4
                stctr += 1
                if psS_gate[pb] is not None:
                    b.wait("tensor", psS_gate[pb][0], psS_gate[pb][1])
                b.inc_on("tensor",
                         lambda e, o=psS[pb][:, j0 * P:(j1 + 1) * P],
                         l=Ks[bkv][:, st * P:(st + 1) * P],
                         r=Qs[bq][:, (4 * g + j0) * P:(4 * g + j1 + 1) * P]:
                         e.matmul(o, l, r, start=True, stop=True),
                         s_mmS, 1)
                need_mask = []
                if 4 * g <= st <= 4 * g + 3 and j0 <= st - 4 * g <= j1:
                    need_mask.append(((st - 4 * g) * P, maskA))
                if window and j0 <= st + 4 - 4 * g <= j1 and st + 4 <= 4 * g + 3:
                    need_mask.append(((st + 4 - 4 * g) * P, maskC))
                if need_mask:
                    b.waitn("vector", s_mmS, s_mmS.n)
                    for (mo, mt) in need_mask:
                        b.op("vector", lambda e, o=psS[pb][:, mo:mo + P],
                             m=mt: e.tensor_tensor(o, o, m[:], op=ALU.add))
                    b.drain_inc("vector", s_msk, 1)
                    b.wait("scalar", s_msk, s_msk.n)
                else:
                    b.waitn("scalar", s_mmS, s_mmS.n)
                if pts_gate[sb] is not None:
                    b.wait("scalar", pts_gate[sb][0], pts_gate[sb][1])
                b.op("scalar", lambda e, o=PTs[sb][:, j0 * P:(j1 + 1) * P],
                     i=psS[pb][:, j0 * P:(j1 + 1) * P]:
                     e.activation(o, i, AF.Exp, bias=b.m16[:], scale=1.0))
                pend.append((st, j0, j1, sb))
                psS_gate[pb] = (s_exp, s_exp.n + 1)
                if len(pend) == 2 or sti == len(sts) - 1:
                    exp_tgt = b.drain_inc("scalar", s_exp, 1)
                    if prev is not None:
                        first = flush(prev[0], prev[1], first)
                    prev = (pend, exp_tgt)
                    pend = []
            first = flush(prev[0], prev[1], first)
            b.waitn("vector", s_mmZ, s_mmZ.n)
            b.op("vector", lambda e, o=denrow, i=psR:
                 e.tensor_copy(o[:], i[0:1, 0:512]))
            dev_t = b.drain_inc("vector", s_dev, 1)
            b.wait("tensor", s_dev, dev_t)
            for j in range(4):
                fn = (lambda e, o=psR[:, 508 + j:509 + j],
                      i=denrow[0:1, j * P:(j + 1) * P]:
                      e.transpose(o, i, b.identf[0:1, 0:1]))
                if j == 3:
                    b.inc_on("tensor", fn, s_tr2, 1)
                else:
                    b.op("tensor", fn)
            b.waitn("vector", s_tr2, s_tr2.n)
            b.op("vector", lambda e, o=dn4, i=psR:
                 e.tensor_copy(o[:], i[:, 508:512]))
            dv2 = b.drain_inc("vector", s_dev, 1)
            b.wait("vector", s_dev, dv2)
            b.op("vector", lambda e, o=rec4, i=dn4: e.reciprocal(o[:], i[:]))
            rec_t = b.drain_inc("vector", s_rec, 1)
            b.wait("tensor", s_rec, rec_t)
            for j in range(4):
                fn = (lambda e, o=psR[0:1, j * P:(j + 1) * P],
                      i=rec4[:, j:j + 1]:
                      e.transpose(o, i, b.identf[:]))
                if j == 3:
                    b.inc_on("tensor", fn, s_tr2, 1)
                else:
                    b.op("tensor", fn)
            b.waitn("vector", s_tr2, s_tr2.n)
            b.op("vector", lambda e, o=rrow, i=psR:
                 e.tensor_copy(o[:], i[0:1, 0:512]))
            rrw_t = b.drain_inc("vector", s_dev, 1)
            b.wait("tensor", s_dev, rrw_t)
            b.inc_on("tensor", lambda e, o=psR, r=rrow:
                     e.matmul(o[:], b.ones1[:], r[0:1, :],
                              start=True, stop=True), s_bc, 1)
            b.waitn("scalar", s_bc, s_bc.n)
            b.op("scalar", lambda e, o=rbst, i=psR:
                 e.activation(o[:], i[:], AF.Copy, bias=0.0, scale=1.0))
            b.drain_inc("scalar", s_evrb, 1)
            psR_gate = (s_evrb, s_evrb.n)
            b.wait("vector", s_evrb, s_evrb.n)
            zsb = g % 2
            if s_stz.n >= 32:
                b.wait("vector", s_stz, s_stz.n - 16)
            b.op("vector", lambda e, o=zst[zsb][:], a=psZ[zb][:], m=rbst[:]:
                 e.tensor_tensor(o, a, m, op=ALU.mult))
            b.drain_inc("vector", s_z, 1)
            psZ_gate[zb] = (s_z, s_z.n)
            b.wait("gpsimd", s_z, s_z.n)
            b.dma("gpsimd", zf[h * P:(h + 1) * P, g * 512:(g + 1) * 512],
                  zst[zsb][:], s_stz)
        qgate[bq] = (s_mmS, s_mmS.n)
        kgate[bkv] = (s_mmS, s_mmS.n)
        vgate[bkv] = (s_mmZ, s_mmZ.n)
        if wst["n"] < 16:
            ready = wdT_tile(b, L, wst, wdTd)
    b._wdT_ready = getattr(b, "_wdT_ready", {})
    b._wdT_ready[L] = ready



def ph_outproj(b, L):
    cfg = CFG[L]
    uid = f"op{L}"
    zf = b.dram(f"zf{L}")
    xin = b.dram("r0" if L == 0 else "r2")
    xout = b.dram("r1" if L == 0 else "r3")

    woT = b.sbuf(f"woT{uid}", [P, 16 * 2048], BF16)
    if not hasattr(b, "dbg"):
        b.dbg = {}
    b.dbg[f"woT{uid}"] = woT
    tpb = TPB(b, uid)
    wt = WT(b, uid, tpb)
    for rr in range(16):
        wt.prep(cfg["wo"], rr * P, P, woT, rr * P, 2048)
    b.wait("tensor", wt.s_evw, wt.s_evw.n)

    zcol = [b.sbuf(f"zc{uid}_{i}", [P, ND * P], BF16) for i in range(2)]
    yrow = [b.sbuf(f"yr{uid}_{i}", [P, D], F32) for i in range(2)]
    xrow = [b.sbuf(f"xr{uid}_{i}", [P, D], F32) for i in range(2)]
    if getattr(b, "_lnjunk", None) is None:
        b._lnjunk = b.sbuf(f"j{uid}", [P, D], F32)
    st = b.sbuf(f"st{uid}", [P, 12], F32)
    psY = [b.psum(f"psY{uid}_{i}", [P, 512], F32) for i in range(4)]

    s_ldz = b.sem(f"ldz{uid}")
    s_ldxr = b.sem(f"ldxr{uid}")
    s_mmY = b.sem(f"mmY{uid}")
    s_evYr = b.sem(f"evYr{uid}")
    s_evYd = b.sem(f"evYd{uid}")
    s_std2 = b.sem(f"std2{uid}")
    s_nrm = b.sem(f"nrm{uid}")
    s_res = b.sem(f"res{uid}")
    s_sto = b.sem(f"sto{uid}")
    b.store_sems.add(s_sto)

    zgate = [None, None]
    ygate = [None, None]
    xgate = [None, None]
    yunit = 0
    for tt in range(NT):
        bi = tt % 2
        stcol = st[:, bi * 6:bi * 6 + 6]
        if zgate[bi] is not None:
            b.wait("sync", zgate[bi][0], zgate[bi][1])
        ldz_t = b.dma("sync",
                      zcol[bi][:].rearrange("p (a t) -> p a t", t=P),
                      zf[:, tt * P:(tt + 1) * P].rearrange(
                          "(a p) t -> p a t", p=P),
                      s_ldz)
        if xgate[bi] is not None:
            b.wait("sync", xgate[bi][0], xgate[bi][1])
        ldx_t = b.dma("sync", xrow[bi][:], xin[tt * P:(tt + 1) * P, :], s_ldxr)
        b.wait("tensor", s_ldz, ldz_t)
        for dcn in range(4):
            bank = psY[yunit % 4]
            if yunit >= 4:
                b.wait("tensor", s_evYr, s_evYr.n - 3)
            for oc in range(16):
                lhsT = zcol[bi][:, oc * P:(oc + 1) * P]
                rhs = woT[:, oc * 2048 + dcn * 512: oc * 2048 + (dcn + 1) * 512]
                fn = (lambda e, o=bank[:], l=lhsT, r=rhs,
                      stf=(oc == 0), sp=(oc == 15):
                      e.matmul(o, l, r, start=stf, stop=sp))
                if oc == 15:
                    b.inc_on("tensor", fn, s_mmY, 1)
                else:
                    b.op("tensor", fn)
            b.waitn("scalar", s_mmY, s_mmY.n)
            if dcn == 0 and ygate[bi] is not None:
                b.wait("scalar", ygate[bi][0], ygate[bi][1])
            b.op("scalar", lambda e,
                 o=yrow[bi][:, dcn * 512:(dcn + 1) * 512],
                 i=bank[:]:
                 e.activation(o, i, AF.Copy, bias=0.0, scale=1.0))
            b.drain_inc("scalar", s_evYr, 1)
            yunit += 1
        zgate[bi] = (s_mmY, s_mmY.n)
        evd_t = b.drain_inc("scalar", s_evYd, 1)
        std_t = ln_epilogue_stats(b, uid, yrow[bi][:], stcol, s_evYd, evd_t,
                                  s_std2)
        b.wait("scalar", s_std2, std_t)
        b.op("scalar", lambda e, y=yrow[bi], sc=stcol:
             e.activation(y[:], y[:], AF.Identity, bias=sc[:, 3:4],
                          scale=sc[:, 5:6]))
        nrm_t = b.drain_inc("scalar", s_nrm, 1)
        b.wait("vector", s_nrm, nrm_t)
        b.wait("vector", s_ldxr, ldx_t)
        b.op("vector", lambda e, x=xrow[bi], y=yrow[bi]:
             e.tensor_tensor(x[:], x[:], y[:], op=ALU.add))
        b.drain_inc("vector", s_res, 1)
        ygate[bi] = (s_res, s_res.n)
        b.wait("gpsimd", s_res, s_res.n)
        b.dma("gpsimd", xout[tt * P:(tt + 1) * P, :], xrow[bi][:], s_sto)
        xgate[bi] = (s_sto, s_sto.n)



class DQ:

    def __init__(self, b, uid, packed_name, scales_name, dper, ngroups,
                 tpb=None):
        self.b = b
        self.uid = uid
        self.tpb = tpb if tpb is not None else TPB(b, uid + "q")
        self.packed = b.dram(packed_name)
        self.scales = b.dram(scales_name)
        self.dper = dper
        self.C = dper // 2
        self.ngr = ngroups
        self.pk = [b.sbuf(f"dqp{uid}_{i}", [P, dper // 8], I32)
                   for i in range(2)]
        self.sets = [tuple(b.sbuf(f"dq{nm}{uid}_{i}", [P, self.C], BF16)
                           for nm in ("a", "b", "c", "d")) for i in range(2)]
        self.u8lo = b.sbuf(f"dql8{uid}", [P, self.C], U8)
        self.u8hi = b.sbuf(f"dqh8{uid}", [P, self.C], U8)
        self.wdeq = b.sbuf(f"dqw{uid}", [P, dper], BF16)
        self.b4m = b.sbuf(f"dq4{uid}", [P, 1], F32)
        self.b12m = b.sbuf(f"dq12{uid}", [P, 1], F32)
        self.sctmp16 = b.sbuf(f"dqs16{uid}", [P, P], F16)
        self.sctmp32 = b.sbuf(f"dqs32{uid}", [P, P], F32)
        self.pssc = self.tpb.banks[0][:].bitcast(F32)
        self.s_ldp = b.sem(f"dqld{uid}")
        self.s_v = b.sem(f"dqv{uid}")
        self.s_a = b.sem(f"dqact{uid}")
        self.s_sc = b.sem(f"dqsc{uid}")
        self.s_tr = b.sem(f"dqtr{uid}")
        self.s_ev = b.sem(f"dqev{uid}")
        self.s_trs = b.sem(f"dqtrs{uid}")
        self.s_evs = b.sem(f"dqevs{uid}")
        self.s_pk = b.sem(f"dqpk{uid}")
        self.s_vg = b.sem(f"dqvg{uid}")
        self.s_gq = b.sem(f"dqgq{uid}")
        self.ctr = 0
        self.tctr = 0
        self.pk_gate = [None, None]
        b.op("vector", lambda e: e.memset(self.b4m[:], -4.0))
        b.op("vector", lambda e: e.memset(self.b12m[:], -12.0))
        ini = b.drain_inc("vector", self.s_v, 1)
        b.wait("scalar", self.s_v, ini)

    def prep_scales(self, scT, n_otiles):
        b = self.b
        prev_trt = None
        for ot in range(n_otiles):
            if prev_trt is not None:
                b.wait("sync", self.s_v, prev_trt)
            ld = b.dma("sync", self.sctmp16[0:self.ngr, :],
                       self.scales[:, ot * P:(ot + 1) * P], self.s_ldp)
            b.wait("vector", self.s_ldp, ld)
            b.op("vector", lambda e: e.tensor_copy(
                self.sctmp32[0:self.ngr, :], self.sctmp16[0:self.ngr, :]))
            trt = b.drain_inc("vector", self.s_v, 1)
            prev_trt = trt
            b.wait("tensor", self.s_v, trt)
            b.wait("tensor", self.s_evs, self.s_evs.n)
            b.inc_on("tensor", lambda e:
                     e.transpose(self.pssc[:, 0:self.ngr],
                                 self.sctmp32[0:self.ngr, :],
                                 b.identf[0:self.ngr, 0:self.ngr]),
                     self.s_trs, 1)
            b.waitn("vector", self.s_trs, self.s_trs.n)
            b.op("vector", lambda e,
                 o=scT[:, ot * self.ngr:(ot + 1) * self.ngr]:
                 e.tensor_copy(o, self.pssc[:, 0:self.ngr]))
            b.drain_inc("vector", self.s_evs, 1)
        b.wait("sync", self.s_evs, self.s_evs.n)
        b.wait("tensor", self.s_evs, self.s_evs.n)

    def prep(self, row0, scT, sc_off, dst, dst_off, ostride, gate=None):
        b = self.b
        pb = self.ctr % 2
        if self.pk_gate[pb] is not None:
            b.wait("sync", self.pk_gate[pb][0], self.pk_gate[pb][1])
        ldt = b.dma("sync", self.pk[pb][:],
                    self.packed[row0:row0 + P, :], self.s_ldp)
        if self.ctr >= 1:
            b.wait("vector", self.tpb.s_tr, self.tpb.s_tr.n)
        b.wait("vector", self.s_ldp, ldt)
        u8 = self.pk[pb][:].bitcast(U8)
        CC = self.C

        def split2(e):
            e.tensor_scalar(self.u8lo[:], u8, 15, None, op0=ALU.bitwise_and)
            e.tensor_scalar(self.u8hi[:], u8, 4, None,
                            op0=ALU.logical_shift_right)
        b.op("vector", split2)
        pkt = b.drain_inc("vector", self.s_pk, 1)
        self.pk_gate[pb] = (self.s_pk, pkt)
        wodd = self.wdeq[:].rearrange("p (n two) -> p two n", two=2)
        for par in range(2):
            src8 = (self.u8lo if par == 0 else self.u8hi)[:]
            a_, b_, c_, d_ = self.sets[par]
            b.op("vector", lambda e, s=src8, a=a_: e.tensor_copy(a[:], s))
            b.op("vector", lambda e, a=a_, bb=b_: e.tensor_scalar(
                bb[:], a[:], 8.0, None, op0=ALU.is_ge))
            b.op("vector", lambda e, a=a_, bb=b_, c=c_:
                 e.scalar_tensor_tensor(c[:], bb[:], -8.0, a[:],
                                        op0=ALU.mult, op1=ALU.add))
            b.op("vector", lambda e, bb=b_: e.tensor_scalar(
                bb[:], bb[:], -1.0, 0.5, op0=ALU.mult, op1=ALU.add))
        vt = b.drain_inc("vector", self.s_v, 1)
        b.wait("scalar", self.s_v, vt)
        for par in range(2):
            a_, b_, c_, d_ = self.sets[par]
            b.op("scalar", lambda e, a=a_, c=c_: e.activation(
                a[:], c[:], AF.Relu, bias=self.b4m[:], scale=1.0))
            b.op("scalar", lambda e, c=c_, d=d_: e.activation(
                d[:], c[:], AF.Relu, bias=self.b12m[:], scale=2.0))
        at = b.drain_inc("scalar", self.s_a, 1)
        b.wait("vector", self.s_a, at)
        for par in range(2):
            a_, b_, c_, d_ = self.sets[par]
            b.op("vector", lambda e, a=a_, c=c_: e.tensor_tensor(
                c[:], c[:], a[:], op=ALU.add))
            b.op("vector", lambda e, c=c_, d=d_: e.tensor_tensor(
                c[:], c[:], d[:], op=ALU.add))
            b.op("vector", lambda e, p=par, bb=b_, c=c_: e.tensor_tensor(
                wodd[:, p, :], c[:], bb[:], op=ALU.mult))
        for gi in range(self.ngr):
            b.op("vector", lambda e, g=gi: e.tensor_scalar(
                self.wdeq[:, g * P:(g + 1) * P],
                self.wdeq[:, g * P:(g + 1) * P],
                scT[:, sc_off + g:sc_off + g + 1], None, op0=ALU.mult))
        dqt = b.drain_inc("vector", self.s_v, 1)
        b.wait("tensor", self.s_v, dqt)
        NDC = self.dper // P
        for g0 in range(0, NDC, 8):
            n = min(8, NDC - g0)
            srcs = [self.wdeq[:, (g0 + i) * P:(g0 + i + 1) * P]
                    for i in range(n)]
            if ostride == P:
                off = dst_off + g0 * P
                dstb = dst[:, off:off + n * P]
            else:
                c0 = g0 + dst_off // ostride
                dstb = dst[:].rearrange("p (c q) -> p c q",
                                        q=ostride)[:, c0:c0 + n, 0:P]
            self.tpb.group(srcs, dstb, gate=gate)
            gate = None
        self.ctr += 1
        return self.tpb.s_ev, self.tpb.s_ev.n



def ph_lngateup(b, L):
    cfg = CFG[L]
    uid = f"gu{L}"
    I = cfg["I"]
    NOC = I // P
    hmd = b.dram(f"hm{L}")
    hfm = b.sbuf(f"hfm{uid}", [P, ND * T], BF16)
    if not hasattr(b, "dbg"):
        b.dbg = {}
    b.dbg[f"hfm{uid}"] = hfm
    tpb = TPB(b, uid)
    s_evh, tgt = emit_ln_to_hfm(b, L, "r1" if L == 0 else "r3", hfm, uid,
                                tpb=tpb)
    b.wait("tensor", s_evh, tgt)

    quant = (L == 0)
    if quant:
        dq = DQ(b, uid, "L0_ffn_gate_up_packed", "L0_ffn_gate_up_scales",
                2048, 16, tpb=tpb)
        scT = b.sbuf(f"scT{uid}", [P, 88 * 16], F32)
        dq.prep_scales(scT, 88)
    else:
        wt = WT(b, uid, tpb)
    wTg = [b.sbuf(f"wTg{uid}_{i}", [P, ND * P], BF16) for i in range(2)]
    wTu = [b.sbuf(f"wTu{uid}_{i}", [P, ND * P], BF16) for i in range(2)]
    gs = b.sbuf(f"gs{uid}", [P, 2048], BF16)
    hmst = [b.sbuf(f"hms{uid}_{i}", [P, 512], BF16) for i in range(2)]
    psG = [b.psum(f"psG{uid}_{i}", [P, 512], F32) for i in range(2)]
    psU = [b.psum(f"psU{uid}_{i}", [P, 512], F32) for i in range(2)]

    s_mmG = b.sem(f"mmG{uid}")
    s_evGr = b.sem(f"evGr{uid}")
    s_mmU = b.sem(f"mmU{uid}")
    s_evU = b.sem(f"evU{uid}")
    s_sthm = b.sem(f"sthm{uid}")
    b.store_sems.add(s_sthm)

    ggate = None
    ugate = None
    ggs = [None, None]
    wggate = [None, None]
    wugate = [None, None]
    hm_ctr = 0

    def prep_oc(oc):
        wb = oc % 2
        if quant:
            dq.prep(oc * P, scT, oc * 16, wTg[wb], 0, P, gate=wggate[wb])
            _, g2 = dq.prep(I + oc * P, scT, (44 + oc) * 16, wTu[wb], 0, P,
                            gate=wugate[wb])
            return (dq.tpb.s_ev, g2)
        wt.prep("L1_ffn_w_gate", oc * P, P, wTg[wb], 0, P, gate=wggate[wb])
        _, g2 = wt.prep("L1_ffn_w_up", oc * P, P, wTu[wb], 0, P,
                        gate=wugate[wb])
        return (wt.s_evw, g2)

    prep_tgt = prep_oc(0)
    for oc in range(NOC):
        wb = oc % 2
        evsem, g2 = prep_tgt
        b.wait("tensor", evsem, g2)
        if oc + 1 < NOC:
            prep_tgt = prep_oc(oc + 1)
        for q4 in range(4):
            qb = q4 % 2
            if ggate is not None:
                b.wait("tensor", ggate[0], ggate[1])
            for dc in range(ND):
                for tg in range(2):
                    col = (q4 * 2 + tg) * 512
                    rhs = hfm[:, dc * T + col: dc * T + col + 512]
                    lhsT = wTg[wb][:, dc * P:(dc + 1) * P]
                    fn = (lambda e, o=psG[tg][:], l=lhsT, r=rhs,
                          stf=(dc == 0), sp=(dc == ND - 1):
                          e.matmul(o, l, r, start=stf, stop=sp))
                    if dc == ND - 1 and tg == 1:
                        b.inc_on("tensor", fn, s_mmG, 1)
                    else:
                        b.op("tensor", fn)
            if ugate is not None:
                b.wait("tensor", ugate[0], ugate[1])
            for dc in range(ND):
                for tg in range(2):
                    col = (q4 * 2 + tg) * 512
                    rhs = hfm[:, dc * T + col: dc * T + col + 512]
                    lhsT = wTu[wb][:, dc * P:(dc + 1) * P]
                    fn = (lambda e, o=psU[tg][:], l=lhsT, r=rhs,
                          stf=(dc == 0), sp=(dc == ND - 1):
                          e.matmul(o, l, r, start=stf, stop=sp))
                    if dc == ND - 1 and tg == 1:
                        b.inc_on("tensor", fn, s_mmU, 1)
                    else:
                        b.op("tensor", fn)
            b.waitn("scalar", s_mmG, s_mmG.n)
            if ggs[qb] is not None:
                b.wait("scalar", ggs[qb][0], ggs[qb][1])
            for tg in range(2):
                b.op("scalar", lambda e,
                     o=gs[:, qb * 1024 + tg * 512: qb * 1024 + (tg + 1) * 512],
                     i=psG[tg][:]:
                     e.activation(o, i, AF.Silu, bias=0.0, scale=1.0))
            gst = b.drain_inc("scalar", s_evGr, 1)
            ggate = (s_evGr, gst)
            b.waitn("vector", s_mmU, s_mmU.n)
            b.wait("vector", s_evGr, gst)
            for tg in range(2):
                sb = hm_ctr % 2
                if hm_ctr >= 2:
                    b.wait("vector", s_sthm, s_sthm.n - 16)
                b.op("vector", lambda e, o=hmst[sb][:], a=psU[tg][:],
                     m=gs[:, qb * 1024 + tg * 512: qb * 1024 + (tg + 1) * 512]:
                     e.tensor_tensor(o, a, m, op=ALU.mult))
                b.drain_inc("vector", s_evU, 1)
                b.wait("gpsimd", s_evU, s_evU.n)
                b.dma("gpsimd", hmd[oc * P:(oc + 1) * P,
                                    (q4 * 2 + tg) * 512:
                                    (q4 * 2 + tg + 1) * 512],
                      hmst[sb][:], s_sthm)
                hm_ctr += 1
            ugate = (s_evU, s_evU.n)
            ggs[qb] = (s_evU, s_evU.n)
        wggate[wb] = (s_mmG, s_mmG.n)
        wugate[wb] = (s_mmU, s_mmU.n)



def ph_down(b, L):
    cfg = CFG[L]
    uid = f"dn{L}"
    I = cfg["I"]
    ICN = I // P
    hmd = b.dram(f"hm{L}")
    wdTd = b.dram(f"wdT{L}")
    xin = b.dram("r1" if L == 0 else "r3")
    xout = b.dram("r2" if L == 0 else "out")

    pre = getattr(b, "_wdT_ready", {}).get(L)
    if pre is None:
        b.push_frame()
        tpb = TPB(b, uid)
        wst = wdT_state(b, L, tpb, uid)
        for ot in range(16):
            pre = wdT_tile(b, L, wst, wdTd)
        b.wait("sync", pre[0], pre[1])
        b.wait("scalar", pre[0], pre[1])
        b.pop_frame()
    else:
        b.wait("sync", pre[0], pre[1])
        b.wait("scalar", pre[0], pre[1])

    hmg = b.sbuf(f"hmg{uid}", [P, ICN * 1024], BF16)
    yg = b.sbuf(f"yg{uid}", [P, 8 * D], BF16)
    ynm = [b.sbuf(f"ynm{uid}_{i}", [P, D], F32) for i in range(2)]
    rwb = [b.sbuf(f"rw{uid}_{i}", [P, 512], BF16) for i in range(4)]
    xrow = [b.sbuf(f"xr{uid}_{i}", [P, D], F32) for i in range(2)]
    if getattr(b, "_lnjunk", None) is None:
        b._lnjunk = b.sbuf(f"j{uid}", [P, D], F32)
    st = b.sbuf(f"st{uid}", [P, 12], F32)
    psD = [b.psum(f"psD{uid}_{i}", [P, 512], F32) for i in range(8)]

    s_ldh = b.sem(f"ldh{uid}")
    s_ldh2 = b.sem(f"ldh2{uid}")
    s_ldw2 = b.sem(f"ldw2{uid}")
    s_ldw3 = b.sem(f"ldw3{uid}")
    s_ldxr = b.sem(f"ldxr{uid}")
    s_mmD = b.sem(f"mmD{uid}")
    s_evDr = b.sem(f"evDr{uid}")
    s_evDd = b.sem(f"evDd{uid}")
    s_std3 = b.sem(f"std3{uid}")
    s_nrm = b.sem(f"nrm{uid}")
    s_res = b.sem(f"res{uid}")
    s_sto = b.sem(f"sto{uid}")
    b.store_sems.add(s_sto)

    hview = hmg[:].rearrange("p (a t) -> p a t", t=1024)
    half = ICN // 2
    hm_gate = None
    rw_gate = [None] * 4
    yg_gate = None
    psgate = None
    stc_gate = [None, None]
    xr_gate = [None, None]
    tc = 0
    for g in range(4):
        if hm_gate is not None:
            b.wait("sync", hm_gate[0], hm_gate[1])
            b.wait("scalar", hm_gate[0], hm_gate[1])
        src = hmd[:, g * 1024:(g + 1) * 1024].rearrange(
            "(a p) t -> p a t", p=P)
        ldh = b.dma("sync", hview[:, 0:half, :], src[:, 0:half, :], s_ldh)
        ldh2 = b.dma("scalar", hview[:, half:ICN, :], src[:, half:ICN, :],
                     s_ldh2)
        b.wait("tensor", s_ldh, ldh)
        b.wait("tensor", s_ldh2, ldh2)
        for dc in range(4):
            if psgate is not None:
                b.wait("tensor", psgate[0], psgate[1])
            if dc == 0 and yg_gate is not None:
                b.wait("scalar", yg_gate[0], yg_gate[1])
            for ic in range(ICN):
                rb = tc % 4
                q = "sync" if tc % 2 == 0 else "scalar"
                qs = s_ldw2 if tc % 2 == 0 else s_ldw3
                tc += 1
                if rw_gate[rb] is not None:
                    b.wait(q, rw_gate[rb][0], rw_gate[rb][1])
                ldw = b.dma(q, rwb[rb][:],
                            wdTd[ic * P:(ic + 1) * P, dc * 512:(dc + 1) * 512],
                            qs)
                b.wait("tensor", qs, ldw)
                for j in range(8):
                    lhsT = hmg[:, ic * 1024 + j * P: ic * 1024 + (j + 1) * P]
                    fn = (lambda e, o=psD[j][:], l=lhsT, r=rwb[rb][:],
                          stf=(ic == 0), sp=(ic == ICN - 1):
                          e.matmul(o, l, r, start=stf, stop=sp))
                    if j == 7:
                        b.inc_on("tensor", fn, s_mmD, 1)
                    else:
                        b.op("tensor", fn)
                rw_gate[rb] = (s_mmD, s_mmD.n)
            b.waitn("scalar", s_mmD, s_mmD.n)
            for j in range(8):
                b.op("scalar", lambda e,
                     o=yg[:, j * D + dc * 512: j * D + (dc + 1) * 512],
                     i=psD[j][:]:
                     e.activation(o, i, AF.Copy, bias=0.0, scale=1.0))
            b.drain_inc("scalar", s_evDr, 8)
            psgate = (s_evDr, s_evDr.n)
        hm_gate = (s_mmD, s_mmD.n)
        evd_t = b.drain_inc("scalar", s_evDd, 1)
        for j in range(8):
            tt = 8 * g + j
            bi = j % 2
            stcol = st[:, bi * 6:bi * 6 + 6]
            if stc_gate[bi] is not None:
                b.wait("scalar", stc_gate[bi][0], stc_gate[bi][1])
                b.wait("vector", stc_gate[bi][0], stc_gate[bi][1])
            if xr_gate[bi] is not None:
                b.wait("sync", xr_gate[bi][0], xr_gate[bi][1])
            ldx_t = b.dma("sync", xrow[bi][:], xin[tt * P:(tt + 1) * P, :],
                          s_ldxr)
            std_t = ln_epilogue_stats(b, uid, yg[:, j * D:(j + 1) * D],
                                      stcol, s_evDd, evd_t, s_std3)
            b.wait("scalar", s_std3, std_t)
            b.op("scalar", lambda e, y=yg[:, j * D:(j + 1) * D],
                 o=ynm[bi][:], sc=stcol:
                 e.activation(o, y, AF.Identity, bias=sc[:, 3:4],
                              scale=sc[:, 5:6]))
            nrm_t = b.drain_inc("scalar", s_nrm, 1)
            b.wait("vector", s_nrm, nrm_t)
            b.wait("vector", s_ldxr, ldx_t)
            b.op("vector", lambda e, x=xrow[bi], y=ynm[bi]:
                 e.tensor_tensor(x[:], x[:], y[:], op=ALU.add))
            b.drain_inc("vector", s_res, 1)
            stc_gate[bi] = (s_res, s_res.n)
            b.wait("gpsimd", s_res, s_res.n)
            b.dma("gpsimd", xout[tt * P:(tt + 1) * P, :], xrow[bi][:], s_sto)
            xr_gate[bi] = (s_sto, s_sto.n)
        yg_gate = (s_res, s_res.n)



PHASES = [
    ("l0_lnqkv", lambda b: ph_lnqkv(b, 0),
     ["r0"], ["qf0", "kf0", "vt0"]),
    ("l0_attn", lambda b: ph_attn(b, 0),
     ["qf0", "kf0", "vt0"], ["zf0"]),
    ("l0_out", lambda b: ph_outproj(b, 0),
     ["zf0", "r0"], ["r1"]),
    ("l0_gateup", lambda b: ph_lngateup(b, 0),
     ["r1"], ["hm0"]),
    ("l0_down", lambda b: ph_down(b, 0),
     ["hm0", "r1"], ["r2", "wdT0"]),
    ("l1_lnqkv", lambda b: ph_lnqkv(b, 1),
     ["r2"], ["qf1", "kf1", "vt1"]),
    ("l1_attn", lambda b: ph_attn(b, 1),
     ["qf1", "kf1", "vt1"], ["zf1"]),
    ("l1_out", lambda b: ph_outproj(b, 1),
     ["zf1", "r2"], ["r3"]),
    ("l1_gateup", lambda b: ph_lngateup(b, 1),
     ["r3"], ["hm1"]),
    ("l1_down", lambda b: ph_down(b, 1),
     ["hm1", "r3"], ["out", "wdT1"]),
]


def build_phase_program(names):
    specs = [p for p in PHASES if p[0] in names]
    reads = set()
    writes = set()
    for (_, _, r, w) in specs:
        reads |= set(r)
        writes |= set(w)
    reads -= writes
    b = B("dev", reads=reads, writes=writes)
    b.store_sems = set()
    emit_consts(b)
    for (_, fn, _, _) in specs:
        fn(b)
    for s in sorted(b.store_sems, key=lambda s: s.name):
        b.wait("gpsimd", s, s.n)
    nc = b.finalize()
    return b, nc


def make_consts():
    ident = np.eye(P, dtype=NPBF)
    i = np.arange(P)
    maskC = np.where(i[None, :] <= i[:, None], 0.0, NEGBIG).astype(np.float32)
    maskA = np.where(i[:, None] <= i[None, :], 0.0, NEGBIG).astype(np.float32)
    half = 64
    inv = 1.0 / (10000.0 ** (np.arange(half, dtype=np.float64) / half))
    t = np.arange(T, dtype=np.float64)
    fr = np.outer(t, inv)
    cos = np.cos(fr).astype(np.float32)
    sin = np.sin(fr).astype(np.float32)
    ropeC = np.zeros((P, NT * 64), np.float32)
    ropeS = np.zeros((P, NT * 64), np.float32)
    for tt in range(NT):
        ropeC[:, tt * 64:(tt + 1) * 64] = cos[tt * P:(tt + 1) * P, :]
        ropeS[:, tt * 64:(tt + 1) * 64] = sin[tt * P:(tt + 1) * P, :]
    return dict(ident=ident, identf=np.eye(P, dtype=np.float32),
                maskC=maskC, maskA=maskA, ropeC=ropeC, ropeS=ropeS)


def build_full_program(upto=None, extra_out=(), extra_in=(), phases=None):
    b = B("full")
    b.store_sems = set()
    b._lnjunk = None
    saved = {}
    for n in extra_out:
        saved[n] = DRAM_SPEC[n]
        DRAM_SPEC[n] = (saved[n][0], saved[n][1], "output")
    for n in extra_in:
        saved[n] = DRAM_SPEC[n]
        DRAM_SPEC[n] = (saved[n][0], saved[n][1], "input")
    emit_consts(b)
    b._lnjunk = b.sbuf("lnjunk_shared", [P, D], F32)
    bar = b.sem("phasebar")
    plist = phases if phases is not None else PHASES[:upto]
    for i, (name, fn, reads, writes) in enumerate(plist):
        b.push_frame()
        fn(b)
        b.pop_frame()
        for s in sorted(b.store_sems, key=lambda s: s.name):
            b.wait("gpsimd", s, s.n)
        tgt = bar.n + 5
        for eng in ENGINES:
            if eng in ("sync", "gpsimd"):
                b.emit(eng, lambda e, h=bar.h: e.sem_inc(h, 1))
            else:
                b.emit(eng, lambda e, h=bar.h: e.drain().then_inc(h, 1))
        bar.inc(5)
        for eng in ENGINES:
            b.wait(eng, bar, tgt)
    nc = b.finalize()
    for n, v in saved.items():
        DRAM_SPEC[n] = v
    return b, nc


_PROG_CACHE = {}


def _get_prog(names):
    key = tuple(names)
    if key not in _PROG_CACHE:
        _PROG_CACHE[key] = build_phase_program(names)
    return _PROG_CACHE[key]


def kernel_multilaunch(**inputs):
    ncores = 8
    consts = make_consts()
    x = np.ascontiguousarray(inputs["L0_attn_x"])
    state = [dict(consts) for _ in range(ncores)]
    for c in range(ncores):
        state[c]["r0"] = np.ascontiguousarray(x[c])
        for k, v in inputs.items():
            if k != "L0_attn_x":
                state[c][k] = np.asarray(v)
    for (name, fn, reads, writes) in PHASES:
        b, nc = _get_prog([name])
        in_maps = [{n: state[c][n] for n in b.ext_in} for c in range(ncores)]
        res = run_bass_kernel_spmd(nc, in_maps, core_ids=list(range(ncores)))
        for c in range(ncores):
            for n in b.ext_out:
                state[c][n] = np.asarray(res.results[c][n])
    return np.stack([state[c]["out"] for c in range(ncores)], axis=0)


_FULL_CACHE = []


def kernel(**inputs):
    ncores = 8
    consts = make_consts()
    x = np.ascontiguousarray(inputs["L0_attn_x"])
    if not _FULL_CACHE:
        _FULL_CACHE.append(build_full_program())
    b, nc = _FULL_CACHE[0]
    in_maps = []
    for c in range(ncores):
        m = dict(consts)
        m["r0"] = np.ascontiguousarray(x[c])
        for k, v in inputs.items():
            if k != "L0_attn_x":
                m[k] = np.asarray(v)
        in_maps.append({n: m[n] for n in b.ext_in})
    res = run_bass_kernel_spmd(nc, in_maps, core_ids=list(range(ncores)))
    return np.stack([np.asarray(res.results[c]["out"]) for c in range(ncores)],
                    axis=0)
